# version 17
# baseline (speedup 1.0000x reference)
import numpy as np
import ml_dtypes
import concourse.bass as bass
import concourse.mybir as mybir
from concourse.bass_utils import run_bass_kernel_spmd
from contextlib import ExitStack

F32 = mybir.dt.float32
BF16 = mybir.dt.bfloat16
AF = mybir.ActivationFunctionType
ALU = mybir.AluOpType
AX = mybir.AxisListType
bf16 = ml_dtypes.bfloat16

NCORES = 8
T = 2048
NT = 16
H = 4096
KH = 32
I = 11008
K_TOP = I // 2
CHW = [512] * 21 + [256]
NCH = len(CHW)
CHOFF = [sum(CHW[:i]) for i in range(NCH)]
IK = 86
NHE = 8
NG = 4
C_IKB = [(0, 12), (12, 12), (24, 12), (36, 12), (48, 12), (60, 12), (72, 12), (84, 2)]
NBLK_HE = len(C_IKB)
N_IT = 16
HI0 = 2.0
LN_EPS = 1e-5
RMS_EPS = 1.1920929e-07
SINK_IT = 50
TAU_INV = 10.0
DISABLE_ILV = False

ROUNDS = []
for _c in range(NCH):
    for _mat, _grp in (("g", 0), ("u", 0), ("g", 1), ("u", 1)):
        ROUNDS.append((_c, _mat, _grp))
NR = len(ROUNDS)
G_IDX = {}
U_IDX = {}
_g = _u = 0
for _r, (_c, _mat, _grp) in enumerate(ROUNDS):
    if _mat == "g":
        G_IDX[_r] = _g; _g += 1
    else:
        U_IDX[_r] = _u; _u += 1
N_GR, N_UR = _g, _u

SINK_SCHED = []
_nd, _na = 2, 1
for _it in range(SINK_IT):
    if _it > 0:
        SINK_SCHED.append(("d_T1u", _it, _nd + 1, _na)); _nd += 1
    SINK_SCHED.append(("a_uexp", _it, (1 if _it == 0 else _nd), _na + 1)); _na += 1
    SINK_SCHED.append(("d_LU", _it, _nd + 1, _na)); _nd += 1
    SINK_SCHED.append(("a_uln", _it, _nd, _na + 1)); _na += 1
    SINK_SCHED.append(("d_T1v", _it, _nd + 1, _na)); _nd += 1
    SINK_SCHED.append(("a_vexp", _it, _nd, _na + 1)); _na += 1
    SINK_SCHED.append(("d_LV", _it, _nd + 1, _na)); _nd += 1
    SINK_SCHED.append(("a_vln", _it, _nd, _na + 1)); _na += 1
SINK_SCHED.append(("d_Tfin", -1, _nd + 1, _na)); _nd += 1
SINK_SCHED.append(("a_hexp", -1, _nd, _na + 1)); _na += 1
SINK_SCHED.append(("a_smax", -1, _nd, _na + 1)); _na += 1
D3_D_FINAL = _nd
D3_A_FINAL = _na


def build_program(debug=()):
    nc = bass.Bass("TRN2", target_bir_lowering=False)

    t_x = nc.dram_tensor("x", [T, H], F32, kind="ExternalInput")
    t_wgT = nc.dram_tensor("wgT", [H, I], F32, kind="ExternalInput")
    t_wuT = nc.dram_tensor("wuT", [H, I], F32, kind="ExternalInput")
    t_wdT = nc.dram_tensor("wdT", [I, H], F32, kind="ExternalInput")
    t_wcT = nc.dram_tensor("wcT", [H, 24], F32, kind="ExternalInput")
    t_wbr = nc.dram_tensor("wbr", [1024, 1024], F32, kind="ExternalInput")
    t_bases = nc.dram_tensor("bases", [128, 24], F32, kind="ExternalInput")
    t_eyeb = nc.dram_tensor("eyeb", [128, 16], F32, kind="ExternalInput")
    t_ones = nc.dram_tensor("onesb", [1, 128], BF16, kind="ExternalInput")
    t_gate = nc.dram_tensor("gatev", [1, 1], F32, kind="ExternalInput")
    t_out = nc.dram_tensor("out", [T, H], F32, kind="ExternalOutput")

    dbg = lambda n: "ExternalOutput" if n in debug else "Internal"
    d_xn = nc.dram_tensor("d_xn", [T, H], BF16, kind=dbg("d_xn"))
    d_wg = nc.dram_tensor("d_wg", [NCH, 4, 8, 128, 512], BF16, kind="Internal")
    d_wu = nc.dram_tensor("d_wu", [NCH, 4, 8, 128, 512], BF16, kind="Internal")
    d_wd = nc.dram_tensor("d_wd", [NHE, IK, 128, 512], BF16, kind="Internal")
    d_wc = nc.dram_tensor("d_wc", [H, 24], BF16, kind="Internal")
    d_wb = nc.dram_tensor("d_wb", [1024, 1024], BF16, kind="Internal")
    d_a = nc.dram_tensor("d_a", [T, I], F32, kind=dbg("d_a"))
    d_p = nc.dram_tensor("d_p", [T, I], BF16, kind=dbg("d_p"))
    d_sm = nc.dram_tensor("d_sm", [T, I], BF16, kind=dbg("d_sm"))
    d_xpre = nc.dram_tensor("d_xpre", [T, 1024], BF16, kind=dbg("d_xpre"))
    d_g = nc.dram_tensor("d_g", [T, H], BF16, kind=dbg("d_g"))

    es = ExitStack()
    sb = lambda n, s, d: es.enter_context(nc.sbuf_tensor(n, s, d))
    sem = lambda n: es.enter_context(nc.semaphore(n))

    s_rrms = sb("s_rrms", [128, NT], F32)
    s_bases = sb("s_bases", [128, 24], F32)
    s_eyeb = sb("s_eyeb", [128, 16], F32)
    s_ones = sb("s_ones", [1, 128], BF16)
    s_st0 = sb("s_st0", [128, 16], F32)
    s_logit = sb("s_logit", [128, NT, 24], F32)
    s_Z = sb("s_Z", [128, NT * 16], F32)
    s_T1 = sb("s_T1", [128, NT * 16], F32)
    s_E = sb("s_E", [128, NT * 16], F32)
    s_u4 = sb("s_u4", [128, NT * 16], F32)
    s_v4 = sb("s_v4", [128, NT * 16], F32)
    s_LUe = sb("s_LUe", [128, NT * 4], F32)
    s_LU = sb("s_LU", [128, NT * 4], F32)
    s_LVe = sb("s_LVe", [128, NT * 4], F32)
    s_LV = sb("s_LV", [128, NT * 4], F32)
    s_Hres = sb("s_Hres", [128, NT * 16], F32)
    s_Hpre = sb("s_Hpre", [128, NT * 4], F32)
    s_Hpost = sb("s_Hpost", [128, NT * 4], F32)
    s_rr1 = sb("s_rr1", [128, NT], F32)
    s_rr2 = sb("s_rr2", [128, NT], F32)
    s_g128 = sb("s_g128", [128, 1], F32)
    s_omg = sb("s_omg", [128, 1], F32)
    s_gtin = sb("s_gtin", [1, 1], F32)
    s_gex = sb("s_gex", [1, 2], F32)
    s_gbf = sb("s_gbf", [1, 1], BF16)
    s_wcat = sb("s_wcat", [128, KH, 24], BF16)
    s_blo = sb("s_blo", [128, 1], F32)
    s_bhi = sb("s_bhi", [128, 1], F32)
    s_bmid = sb("s_bmid", [128, 1], F32)
    s_bcnt = sb("s_bcnt", [128, 1], F32)
    s_bsel = sb("s_bsel", [128, 1], F32)
    s_bd1 = sb("s_bd1", [128, 1], F32)
    s_bd2 = sb("s_bd2", [128, 1], F32)
    s_bstp = sb("s_bstp", [128, 1], F32)

    pb = [es.enter_context(nc.psum_tensor(f"pb{b}", [128, 512], F32))
          for b in range(8)]

    p0_x = sem("p0_x"); p0_v = sem("p0_v"); p0_r = sem("p0_r")
    p0_xn = sem("p0_xn"); p0_xst = sem("p0_xst")
    x_t = sem("x_t"); w_cast = sem("w_cast"); k_cst = sem("k_cst")
    a_strip = sem("a_strip"); a_mm = sem("a_mm")
    a_slr = sem("a_slr"); a_ar = sem("a_ar"); a_ast = sem("a_ast")
    a_pr = sem("a_pr"); a_pst = sem("a_pst")
    d1_mm = sem("d1_mm"); d1_ev = sem("d1_ev"); d_wl = sem("d_wl")
    d3_d = sem("d3_d"); d3_a = sem("d3_a")
    d4_ld = sem("d4_ld"); d4_ps = sem("d4_ps"); d4_tl = sem("d4_tl")
    d4_ymm = sem("d4_ymm"); d4_yev = sem("d4_yev")
    d5_f = sem("d5_f"); d5_st = sem("d5_st")
    b_ld = sem("b_ld"); b_mid = sem("b_mid"); b_cnt = sem("b_cnt")
    b_smr = sem("b_smr"); b_smst = sem("b_smst")
    c_smld = sem("c_smld"); c_wd = sem("c_wd"); c_blk = sem("c_blk")
    c_ev = sem("c_ev"); c_fin = sem("c_fin")
    c_xg0 = sem("c_xg0"); c_xg1 = sem("c_xg1")
    c_os0 = sem("c_os0"); c_os1 = sem("c_os1"); c_os2 = sem("c_os2")
    b_lda = sem("b_lda"); b_lp0 = sem("b_lp0"); b_lp1 = sem("b_lp1")
    b_st0 = sem("b_st0"); b_st1 = sem("b_st1")
    out_st = sem("out_st")

    W_WC = 16
    W_WB = 32
    W_CHUNK = lambda c: 32 + 16 * 2 * (c + 1)
    W_ALL = 16 * (2 + 2 * NCH + IK)

    with (nc.sbuf_tensor("s_x0", [128, 2, H], F32) as s_x0,
          nc.sbuf_tensor("s_xnb", [128, 2, H], BF16) as s_xnb,
          nc.sbuf_tensor("s_sq", [128, H], BF16) as s_sq,
          nc.Block() as block):

        @block.gpsimd
        def _(gpsimd):
            gpsimd.dma_start(d_wc[:, :], t_wcT[:, :]).then_inc(w_cast, 16)
            gpsimd.wait_ge(w_cast, 16)
            gpsimd.dma_start(d_wb[:, :], t_wbr[:, :]).then_inc(w_cast, 16)
            gpsimd.wait_ge(w_cast, 32)
            for c in range(NCH):
                W = CHW[c]
                for tW, dW in ((t_wgT, d_wg), (t_wuT, d_wu)):
                    gpsimd.dma_start(
                        dW[c, :, :, :, 0:W].rearrange("kb k p w -> (kb k p) w"),
                        tW[:, CHOFF[c]:CHOFF[c] + W]).then_inc(w_cast, 16)
                gpsimd.wait_ge(w_cast, 32 + 16 * 2 * (c + 1))
            for ik in range(IK):
                gpsimd.dma_start(
                    d_wd[:, ik, :, :],
                    t_wdT[ik * 128:(ik + 1) * 128, :].rearrange(
                        "p (he w) -> he p w", he=NHE)).then_inc(w_cast, 16)


        @block.sync
        def _(sync):
            sync.dma_start(s_bases[:, :], t_bases[:, :]).then_inc(k_cst, 16)
            sync.dma_start(s_eyeb[:, :], t_eyeb[:, :]).then_inc(k_cst, 16)
            sync.dma_start(s_ones[:, :], t_ones[:, :]).then_inc(k_cst, 16)
            for m in range(NT):
                if m >= 2:
                    sync.wait_ge(p0_xn, m - 1)
                sync.dma_start(s_x0[:, m % 2, :],
                               t_x[m * 128:(m + 1) * 128, :]).then_inc(p0_x, 16)

        @block.vector
        def _(vector):
            INV_H = 1.0 / H
            for m in range(NT):
                vector.wait_ge(p0_x, 16 * (m + 1))
                xa = s_x0[:, m % 2, :]
                vector.reduce_sum(s_st0[:, 0:1], xa, AX.X)
                vector.tensor_tensor(s_sq[:, :], xa, xa, ALU.mult)
                vector.drain()
                vector.reduce_sum(s_st0[:, 1:2], s_sq[:, :], AX.X)
                vector.drain()
                vector.tensor_scalar(s_st0[:, 2:3], s_st0[:, 0:1], INV_H, None, ALU.mult)
                vector.tensor_scalar(s_st0[:, 3:4], s_st0[:, 1:2], INV_H, None, ALU.mult)
                vector.drain()
                vector.tensor_tensor(s_st0[:, 4:5], s_st0[:, 2:3], s_st0[:, 2:3], ALU.mult)
                vector.drain()
                vector.tensor_tensor(s_st0[:, 5:6], s_st0[:, 3:4], s_st0[:, 4:5], ALU.subtract)
                vector.drain()
                vector.tensor_scalar(s_st0[:, 6:7], s_st0[:, 5:6], LN_EPS, None, ALU.add)
                vector.drain()
                vector.reciprocal(s_st0[:, 7:8], s_st0[:, 6:7])
                vector.drain()
                vector.tensor_tensor(s_st0[:, 8:9], s_st0[:, 5:6], s_st0[:, 7:8], ALU.mult)
                vector.drain()
                vector.tensor_scalar(s_st0[:, 9:10], s_st0[:, 8:9], RMS_EPS, None, ALU.add)
                vector.drain()
                vector.engine_nop().then_inc(p0_v, 1)
                vector.wait_ge(p0_r, m + 1)
                if m >= 2:
                    vector.wait_ge(p0_xst, 16 * (m - 1))
                vector.tensor_scalar(s_xnb[:, m % 2, :], xa,
                                     s_st0[:, 2:3], s_st0[:, 10:11],
                                     ALU.subtract, ALU.mult)
                vector.drain()
                vector.engine_nop().then_inc(p0_xn, 1)

        @block.scalar
        def _(scalar):
            for m in range(NT):
                scalar.wait_ge(p0_v, m + 1)
                scalar.activation(s_st0[:, 11:12], s_st0[:, 6:7], AF.Ln)
                scalar.drain()
                scalar.activation(s_st0[:, 10:11], s_st0[:, 11:12], AF.Exp, scale=-0.5)
                scalar.drain()
                scalar.activation(s_st0[:, 12:13], s_st0[:, 9:10], AF.Ln)
                scalar.drain()
                scalar.activation(s_rrms[:, m:m + 1], s_st0[:, 12:13], AF.Exp, scale=-0.5)
                scalar.drain()
                scalar.nop().then_inc(p0_r, 1)
                scalar.wait_ge(p0_xn, m + 1)
                scalar.dma_start(d_xn[m * 128:(m + 1) * 128, :],
                                 s_xnb[:, m % 2, :]).then_inc(p0_xst, 16)

    with (nc.sbuf_tensor("s_xnT", [128, KH, T], BF16) as s_xnT,
          nc.sbuf_tensor("s_strip", [128, 6, 8, 512], BF16) as s_strip,
          nc.sbuf_tensor("s_sl", [128, 8, 512], F32) as s_sl,
          nc.sbuf_tensor("s_ppc", [128, 2, 512], BF16) as s_ppc,
          nc.Block() as block):

        @block.sync
        def _(sync):
            sync.wait_ge(p0_xst, 16 * NT)
            for k in range(KH):
                sync.dma_start(s_xnT[:, k, :],
                               d_xn[:, k * 128:(k + 1) * 128],
                               transpose=True).then_inc(x_t, 16)
            blk_idx = 0
            for r, (c, mat, grp) in enumerate(ROUNDS):
                W = CHW[c]
                dW = d_wg if mat == "g" else d_wu
                for kb in range(4):
                    slot = blk_idx % 6
                    if blk_idx >= 6:
                        sync.wait_ge(a_mm, 8 * ((blk_idx - 6) // 4 + 1))
                    sync.wait_ge(w_cast, W_CHUNK(c))
                    sync.dma_start(
                        s_strip[:, slot, :, 0:W],
                        dW[c, kb, :, :, 0:W].rearrange("k p w -> p k w"),
                    ).then_inc(a_strip, 16)
                    blk_idx += 1

        @block.tensor
        def _(tensor):
            tensor.wait_ge(x_t, 16 * KH)
            blk_idx = 0
            for r, (c, mat, grp) in enumerate(ROUNDS):
                W = CHW[c]
                for kb in range(4):
                    tensor.wait_ge(a_strip, 16 * (blk_idx + 1))
                    slot = blk_idx % 6
                    for ts_ in range(8):
                        t_glob = grp * 8 + ts_
                        bank = ts_
                        if kb == 0 and r >= 1:
                            pr = r - 1
                            if ROUNDS[pr][1] == "g":
                                tensor.wait_ge(a_slr, 8 * G_IDX[pr] + ts_ + 1)
                            else:
                                tensor.wait_ge(a_pr, 8 * U_IDX[pr] + ts_ + 1)
                        for kk in range(8):
                            k = kb * 8 + kk
                            mm = tensor.matmul(
                                pb[bank][:, 0:W],
                                s_xnT[:, k, t_glob * 128:(t_glob + 1) * 128],
                                s_strip[:, slot, kk, 0:W],
                                start=(kb == 0 and kk == 0),
                                stop=(kb == 3 and kk == 7))
                            if kb == 3 and kk == 7:
                                mm.then_inc(a_mm, 1)
                    blk_idx += 1
            tensor.wait_ge(a_slr, 8 * N_GR)
            tensor.wait_ge(a_pr, 8 * N_UR)
            tensor.wait_ge(d_wl, 16)
            for m in range(NT):
                bank = m % 4
                if m >= 4:
                    tensor.wait_ge(d1_ev, m - 3)
                for k in range(KH):
                    mm = tensor.matmul(
                        pb[bank][:, 0:24],
                        s_xnT[:, k, m * 128:(m + 1) * 128],
                        s_wcat[:, k, :],
                        start=(k == 0), stop=(k == KH - 1))
                    if k == KH - 1:
                        mm.then_inc(d1_mm, 1)

        @block.vector
        def _(vector):
            ppc_i = 0
            for r, (c, mat, grp) in enumerate(ROUNDS):
                W = CHW[c]
                if mat == "g":
                    continue
                for ts_ in range(8):
                    bank = ts_
                    vector.wait_ge(a_mm, 8 * r + ts_ + 1)
                    vector.wait_ge(a_slr, 8 * G_IDX[r - 1] + ts_ + 1)
                    if ppc_i >= 2:
                        vector.wait_ge(a_pst, 16 * (ppc_i - 1))
                    vector.tensor_tensor(
                        s_ppc[:, ppc_i % 2, 0:W],
                        s_sl[:, ts_, 0:W], pb[bank][:, 0:W], ALU.mult)
                    vector.drain()
                    vector.engine_nop().then_inc(a_pr, 1)
                    ppc_i += 1
                    vector.scalar_tensor_tensor(
                        s_sl[:, ts_, 0:W], s_sl[:, ts_, 0:W], -1.0,
                        s_sl[:, ts_, 0:W], ALU.mult, ALU.max)
                    vector.drain()
                    vector.engine_nop().then_inc(a_ar, 1)
            for m in range(NT):
                vector.wait_ge(d1_mm, m + 1)
                vector.tensor_copy(s_logit[:, m, :], pb[m % 4][:, 0:24])
                vector.drain()
                vector.engine_nop().then_inc(d1_ev, 1)

        @block.scalar
        def _(scalar):
            apc_i = ppc_i = 0
            for r, (c, mat, grp) in enumerate(ROUNDS):
                W = CHW[c]
                pair = 2 * c + grp
                for ts_ in range(8):
                    bank = ts_
                    t_glob = grp * 8 + ts_
                    if mat == "g":
                        scalar.wait_ge(a_mm, 8 * r + ts_ + 1)
                        if pair >= 1:
                            scalar.wait_ge(a_ast, 16 * (8 * (pair - 1) + ts_ + 1))
                        scalar.activation(s_sl[:, ts_, 0:W], pb[bank][:, 0:W],
                                          AF.Silu)
                        scalar.drain()
                        scalar.nop().then_inc(a_slr, 1)
                    else:
                        scalar.wait_ge(a_ar, apc_i + 1)
                        scalar.dma_start(
                            d_a[t_glob * 128:(t_glob + 1) * 128,
                                CHOFF[c]:CHOFF[c] + W],
                            s_sl[:, ts_, 0:W]).then_inc(a_ast, 16)
                        apc_i += 1
                        scalar.wait_ge(a_pr, ppc_i + 1)
                        scalar.dma_start(
                            d_p[t_glob * 128:(t_glob + 1) * 128,
                                CHOFF[c]:CHOFF[c] + W],
                            s_ppc[:, ppc_i % 2, 0:W]).then_inc(a_pst, 16)
                        ppc_i += 1
            scalar.wait_ge(w_cast, W_WC)
            scalar.dma_start(s_wcat[:, :, :],
                             d_wc[:, :].rearrange("(k p) c -> p k c", p=128)
                             ).then_inc(d_wl, 16)

    IQ = I // 8
    NPQ = 8
    s_ba = nc.alloc_sbuf_tensor("s_ba", [128, I], F32)
    s_bpr = nc.alloc_sbuf_tensor("s_bpr", [128, 2, IQ], BF16)
    s_bsr = nc.alloc_sbuf_tensor("s_bsr", [128, 2, IQ], BF16)
    IH = I // 2
    s_btr = nc.alloc_sbuf_tensor("s_btr", [128, IH], BF16)

    with (nc.sbuf_tensor("s_wbrs", [128, 8, 1024], BF16) as s_wbrs,
          nc.sbuf_tensor("s_xnm", [128, H], BF16) as s_xnm,
          nc.sbuf_tensor("s_xpm", [128, 1024], BF16) as s_xpm,
          nc.sbuf_tensor("s_xpT", [128, 8, 128], BF16) as s_xpT,
          nc.sbuf_tensor("s_ym", [128, 1024], BF16) as s_ym,
          nc.sbuf_tensor("s_fm", [128, H], F32) as s_fm,
          nc.sbuf_tensor("s_gm", [128, H], BF16) as s_gm,
          nc.Block() as block):

        @block.gpsimd
        def _(gpsimd):
            for m in range(NT):
                if m >= 1:
                    gpsimd.wait_ge(d5_f, m)
                else:
                    gpsimd.wait_ge(a_ast, 16 * N_GR * 8)
                    gpsimd.wait_ge(a_pst, 16 * N_UR * 8)
                gpsimd.dma_start(
                    s_xnm[:, :],
                    d_xn[m * 128:(m + 1) * 128, :]).then_inc(d4_ld, 16)

        @block.sync
        def _(sync):
            for m in range(NT):
                sync.wait_ge(d5_st, 16 * (2 * m + 1))
                if m >= 1:
                    sync.wait_ge(d4_ymm, 2 * m)
                for ck in range(8):
                    sync.dma_start(
                        s_xpT[:, ck, :],
                        d_xpre[m * 128:(m + 1) * 128,
                               ck * 128:(ck + 1) * 128],
                        transpose=True).then_inc(d4_tl, 16)

        @block.tensor
        def _(tensor):
            tensor.wait_ge(k_cst, 48)
            tensor.wait_ge(d3_d, 2)
            tensor.matmul(pb[6][:, 0:1], s_ones[:, :], s_gbf[:, :],
                          start=True, stop=True).then_inc(d1_mm, 1)
            tensor.wait_ge(d_wl, 32)
            for m in range(NT):
                tensor.wait_ge(d4_tl, 16 * 8 * (m + 1))
                if m >= 1:
                    tensor.wait_ge(d4_yev, m)
                for half in range(2):
                    bank = 4 + half
                    for ck in range(8):
                        mm = tensor.matmul(
                            pb[bank][:, :],
                            s_xpT[:, ck, :],
                            s_wbrs[:, ck, half * 512:(half + 1) * 512],
                            start=(ck == 0), stop=(ck == 7))
                        if ck == 7:
                            mm.then_inc(d4_ymm, 1)

        @block.vector
        def _(vector):
            vector.wait_ge(d1_ev, NT)
            vector.wait_ge(k_cst, 48)
            for m in range(NT):
                vector.tensor_scalar(
                    s_Z[:, m * 16:(m + 1) * 16], s_logit[:, m, 0:16],
                    s_rrms[:, m:m + 1], TAU_INV, ALU.mult, ALU.mult)
                vector.tensor_scalar(
                    s_Hpre[:, m * 4:(m + 1) * 4], s_logit[:, m, 16:20],
                    s_rrms[:, m:m + 1], None, ALU.mult)
                vector.tensor_scalar(
                    s_Hpost[:, m * 4:(m + 1) * 4], s_logit[:, m, 20:24],
                    s_rrms[:, m:m + 1], None, ALU.mult)
            vector.drain()
            vector.tensor_scalar(s_E[:, 0:16], s_bases[:, 0:16], TAU_INV, None, ALU.mult)
            vector.drain()
            Zg = s_Z.rearrange("p (g ij) -> p g ij", ij=16)
            HpreG = s_Hpre.rearrange("p (g j) -> p g j", j=4)
            HpostG = s_Hpost.rearrange("p (g j) -> p g j", j=4)
            bview = s_E[:, 0:16].unsqueeze(1).to_broadcast([128, NT, 16])
            vector.tensor_tensor(Zg, Zg, bview, ALU.add)
            pview = s_bases[:, 16:20].unsqueeze(1).to_broadcast([128, NT, 4])
            vector.tensor_tensor(HpreG, HpreG, pview, ALU.add)
            qview = s_bases[:, 20:24].unsqueeze(1).to_broadcast([128, NT, 4])
            vector.tensor_tensor(HpostG, HpostG, qview, ALU.add)
            vector.drain()
            vector.engine_nop().then_inc(d3_d, 1)
            vector.wait_ge(d3_a, 1)
            vector.tensor_scalar(s_gex[:, 1:2], s_gex[:, 0:1], 1.0, None, ALU.add)
            vector.drain()
            vector.reciprocal(s_gtin[:, :], s_gex[:, 1:2])
            vector.drain()
            vector.tensor_copy(s_gbf[:, :], s_gtin[:, :])
            vector.drain()
            vector.engine_nop().then_inc(d3_d, 1)
            E4 = s_E.rearrange("p (g i j) -> p g i j", i=4, j=4)
            u44 = s_u4.rearrange("p (g i j) -> p g i j", i=4, j=4)
            v44 = s_v4.rearrange("p (g i j) -> p g i j", i=4, j=4)
            LUg = s_LU.rearrange("p (g i) -> p g i", i=4)
            LUeg = s_LUe.rearrange("p (g i) -> p g i", i=4)
            LVg = s_LV.rearrange("p (g j) -> p g j", j=4)
            LVeg = s_LVe.rearrange("p (g j) -> p g j", j=4)
            for step, it, ndv, nav in SINK_SCHED:
                if step == "d_T1u":
                    vector.wait_ge(d3_a, nav)
                    for jj in range(4):
                        vector.tensor_copy(v44[:, :, jj, :], LVg[:, :, :])
                    vector.drain()
                    vector.tensor_tensor(s_T1[:, :], s_Z[:, :], s_v4[:, :], ALU.subtract)
                    vector.drain()
                    vector.engine_nop().then_inc(d3_d, 1)
                elif step == "d_LU":
                    vector.wait_ge(d3_a, nav)
                    vector.reduce_sum(LUeg[:, :, :], E4[:, :, :, :], AX.X)
                    vector.drain()
                    vector.engine_nop().then_inc(d3_d, 1)
                elif step == "d_T1v":
                    vector.wait_ge(d3_a, nav)
                    for jj in range(4):
                        vector.tensor_copy(u44[:, :, :, jj], LUg[:, :, :])
                    vector.drain()
                    vector.tensor_tensor(s_T1[:, :], s_Z[:, :], s_u4[:, :], ALU.subtract)
                    vector.drain()
                    vector.engine_nop().then_inc(d3_d, 1)
                elif step == "d_LV":
                    vector.wait_ge(d3_a, nav)
                    vector.tensor_reduce(LVeg[:, :, :],
                                         E4[:, :, :, :].transpose([0, 1, 3, 2]),
                                         AX.X, ALU.add)
                    vector.drain()
                    vector.engine_nop().then_inc(d3_d, 1)
                elif step == "d_Tfin":
                    vector.wait_ge(d3_a, nav)
                    for jj in range(4):
                        vector.tensor_copy(v44[:, :, jj, :], LVg[:, :, :])
                    vector.drain()
                    vector.tensor_tensor(s_T1[:, :], s_Z[:, :], s_u4[:, :], ALU.subtract)
                    vector.drain()
                    vector.tensor_tensor(s_T1[:, :], s_T1[:, :], s_v4[:, :], ALU.subtract)
                    vector.drain()
                    vector.engine_nop().then_inc(d3_d, 1)
            vector.wait_ge(d1_mm, NT + 1)
            vector.tensor_copy(s_g128[:, :], pb[6][:, 0:1])
            vector.drain()
            vector.tensor_scalar(s_omg[:, :], s_g128[:, :], -1.0, 1.0, ALU.mult, ALU.add)
            vector.drain()
            vector.wait_ge(d3_a, D3_A_FINAL - 1)
            vector.tensor_scalar(s_Hres[:, :], s_Hres[:, :], s_g128[:, 0:1], None, ALU.mult)
            vector.drain()
            HresG = s_Hres.rearrange("p (g ij) -> p g ij", ij=16)
            eview = s_eyeb[:, :].unsqueeze(1).to_broadcast([128, NT, 16])
            vector.scalar_tensor_tensor(HresG, eview, s_omg[:, 0:1], HresG,
                                        ALU.mult, ALU.add)
            vector.drain()
            vector.wait_ge(d3_a, D3_A_FINAL)
            vector.reduce_sum(s_rr1[:, :], HpreG, AX.X)
            vector.reduce_sum(s_rr2[:, :], HpostG, AX.X)
            vector.drain()
            vector.reciprocal(s_rr1[:, :], s_rr1[:, :])
            vector.reciprocal(s_rr2[:, :], s_rr2[:, :])
            vector.drain()
            rpre = s_rr1[:, :].unsqueeze(2).to_broadcast([128, NT, 4])
            rpost = s_rr2[:, :].unsqueeze(2).to_broadcast([128, NT, 4])
            vector.tensor_tensor(HpreG, HpreG, rpre, ALU.mult)
            vector.tensor_tensor(HpostG, HpostG, rpost, ALU.mult)
            vector.drain()
            HresQ = s_Hres.rearrange("p (g i j) -> p g i j", i=4, j=4)
            for m in range(NT):
                vector.wait_ge(d4_ld, 16 * (m + 1))
                if m >= 1:
                    vector.wait_ge(d5_st, 16 * (2 * m))
                vector.tensor_scalar(s_xpm[:, :], s_xnm[:, 0:1024],
                                     HpreG[:, m, 0:1], None, ALU.mult)
                vector.drain()
                for n in range(1, 4):
                    vector.scalar_tensor_tensor(
                        s_xpm[:, :], s_xnm[:, n * 1024:(n + 1) * 1024],
                        HpreG[:, m, n:n + 1], s_xpm[:, :], ALU.mult, ALU.add)
                    vector.drain()
                vector.engine_nop().then_inc(d4_ps, 1)
                vector.wait_ge(d4_ymm, 2 * (m + 1))
                vector.tensor_copy(s_ym[:, 0:512], pb[4][:, :])
                vector.tensor_copy(s_ym[:, 512:1024], pb[5][:, :])
                vector.drain()
                vector.engine_nop().then_inc(d4_yev, 1)
                for i2 in range(4):
                    fsl = s_fm[:, i2 * 1024:(i2 + 1) * 1024]
                    vector.tensor_scalar(fsl, s_xnm[:, 0:1024],
                                         HresQ[:, m, i2, 0:1], None, ALU.mult)
                    vector.drain()
                    for j2 in range(1, 4):
                        vector.scalar_tensor_tensor(
                            fsl, s_xnm[:, j2 * 1024:(j2 + 1) * 1024],
                            HresQ[:, m, i2, j2:j2 + 1], fsl, ALU.mult, ALU.add)
                        vector.drain()
                    vector.scalar_tensor_tensor(
                        fsl, s_ym[:, :], HpostG[:, m, i2:i2 + 1], fsl,
                        ALU.mult, ALU.add)
                    vector.drain()
                vector.engine_nop().then_inc(d5_f, 1)
        @block.scalar
        def _(scalar):
            scalar.wait_ge(w_cast, W_WB)
            scalar.dma_start(s_wbrs[:, :, :],
                             d_wb[:, :].rearrange("(k p) c -> p k c", p=128)
                             ).then_inc(d_wl, 16)
            scalar.dma_start(s_gtin[:, :], t_gate[:, :]).then_inc(d_wl, 16)
            scalar.wait_ge(d_wl, 48)
            scalar.activation(s_gex[:, 0:1], s_gtin[:, :], AF.Exp, scale=-1.0)
            scalar.drain()
            scalar.nop().then_inc(d3_a, 1)
            for step, it, ndv, nav in SINK_SCHED:
                if step == "a_uexp":
                    scalar.wait_ge(d3_d, ndv)
                    src = s_Z if it == 0 else s_T1
                    scalar.activation(s_E[:, :], src[:, :], AF.Exp)
                    scalar.drain()
                    scalar.nop().then_inc(d3_a, 1)
                elif step == "a_uln":
                    scalar.wait_ge(d3_d, ndv)
                    scalar.activation(s_LU[:, :], s_LUe[:, :], AF.Ln)
                    scalar.drain()
                    scalar.nop().then_inc(d3_a, 1)
                elif step == "a_vexp":
                    scalar.wait_ge(d3_d, ndv)
                    scalar.activation(s_E[:, :], s_T1[:, :], AF.Exp)
                    scalar.drain()
                    scalar.nop().then_inc(d3_a, 1)
                elif step == "a_vln":
                    scalar.wait_ge(d3_d, ndv)
                    scalar.activation(s_LV[:, :], s_LVe[:, :], AF.Ln)
                    scalar.drain()
                    scalar.nop().then_inc(d3_a, 1)
                elif step == "a_hexp":
                    scalar.wait_ge(d3_d, ndv)
                    scalar.activation(s_Hres[:, :], s_T1[:, :], AF.Exp)
                    scalar.drain()
                    scalar.nop().then_inc(d3_a, 1)
                elif step == "a_smax":
                    scalar.activation(s_Hpre[:, :], s_Hpre[:, :], AF.Exp)
                    scalar.drain()
                    scalar.activation(s_Hpost[:, :], s_Hpost[:, :], AF.Exp)
                    scalar.drain()
                    scalar.nop().then_inc(d3_a, 1)
            for m in range(NT):
                scalar.wait_ge(d4_ps, m + 1)
                scalar.dma_start(d_xpre[m * 128:(m + 1) * 128, :],
                                 s_xpm[:, :]).then_inc(d5_st, 16)
                scalar.wait_ge(d5_f, m + 1)
                scalar.activation(s_gm[:, :], s_fm[:, :], AF.Silu)
                scalar.drain()
                scalar.dma_start(d_g[m * 128:(m + 1) * 128, :],
                                 s_gm[:, :]).then_inc(d5_st, 16)
    NPIECE = NG * NHE * 4
    N_ODD = N_IT // 2

    with (nc.sbuf_tensor("s_smT", [128, IK, 512], BF16) as s_smT,
          nc.sbuf_tensor("s_wdb", [128, 2, 12, 512], BF16) as s_wdb,
          nc.sbuf_tensor("s_xpc", [128, 2, 512], F32) as s_xpc,
          nc.sbuf_tensor("s_gpc", [128, 2, 512], BF16) as s_gpc,
          nc.sbuf_tensor("s_outc", [128, 3, 512], F32) as s_outc,
          nc.sbuf_tensor("s_t1c", [128, 2, 512], F32) as s_t1c,
          nc.Block() as block):

        @block.sync
        def _(sync):
            blkj = 0
            for gidx in range(NG):
                npieces = 4 * NPQ * (gidx + 1)
                sync.wait_ge(b_st0, 16 * (npieces // 2))
                sync.wait_ge(b_st1, 16 * (npieces // 2))
                if gidx >= 1:
                    sync.wait_ge(c_blk, NBLK_HE * NHE * gidx)
                for ik in range(IK):
                    sync.dma_start(
                        s_smT[:, ik, :],
                        d_sm[gidx * 512:(gidx + 1) * 512,
                             ik * 128:(ik + 1) * 128],
                        transpose=True).then_inc(c_smld, 16)
                for he in range(NHE):
                    for bi, (ik0, nik) in enumerate(C_IKB):
                        if blkj >= 2:
                            sync.wait_ge(c_blk, blkj - 1)
                        if blkj == 0:
                            sync.wait_ge(w_cast, W_ALL)
                        sync.dma_start(
                            s_wdb[:, blkj % 2, 0:nik, :],
                            d_wd[he, ik0:ik0 + nik, :, :].rearrange(
                                "ik p w -> p ik w")).then_inc(c_wd, 16)
                        blkj += 1

        @block.tensor
        def _(tensor):
            blkj = 0
            for gidx in range(NG):
                tensor.wait_ge(c_smld, 16 * IK * (gidx + 1))
                for he in range(NHE):
                    heg = gidx * NHE + he
                    bankbase = 0 if he % 2 == 0 else 4
                    for bi, (ik0, nik) in enumerate(C_IKB):
                        tensor.wait_ge(c_wd, 16 * (blkj + 1))
                        bslot = blkj % 2
                        for ts_ in range(4):
                            bank = bankbase + ts_
                            if bi == 0 and heg >= 2:
                                tensor.wait_ge(c_ev, 4 * (heg - 2) + ts_ + 1)
                            for ii in range(nik):
                                ik = ik0 + ii
                                mm = tensor.matmul(
                                    pb[bank][:, :],
                                    s_smT[:, ik, ts_ * 128:(ts_ + 1) * 128],
                                    s_wdb[:, bslot, ii, :],
                                    start=(bi == 0 and ii == 0),
                                    stop=(bi == NBLK_HE - 1 and ii == nik - 1))
                                if ts_ == 0 and ii == 0 and bi >= 1:
                                    mm.then_inc(c_blk, 1)
                                if (bi == NBLK_HE - 1 and ts_ == 3
                                        and ii == nik - 1):
                                    mm.then_inc(c_blk, 1)
                        blkj += 1

        @block.vector
        def _(vector):
            state = {"piece": 0, "nbcnt": 0, "nbmid": 0, "nsmr": 0}

            def emit_evac():
                p = state["piece"]
                heg = p // 4
                he = heg % NHE
                ts_ = p % 4
                bankbase = 0 if he % 2 == 0 else 4
                bank = bankbase + ts_
                vector.wait_ge(c_blk, NBLK_HE * (heg + 1))
                vector.wait_ge(c_xg0 if p % 2 == 0 else c_xg1,
                               32 * (p // 2 + 1))
                if p >= 3:
                    sl3 = p % 3
                    vector.wait_ge([c_os0, c_os1, c_os2][sl3],
                                   16 * (p // 3))
                vector.tensor_tensor(s_t1c[:, p % 2, :], pb[bank][:, :],
                                     s_gpc[:, p % 2, :], ALU.mult).then_inc(c_ev, 1)
                vector.drain()
                vector.tensor_tensor(s_outc[:, p % 3, :], s_t1c[:, p % 2, :],
                                     s_xpc[:, p % 2, :], ALU.add).then_inc(c_fin, 1)
                vector.drain()
                state["piece"] += 1

            def pieces_unlocked(m):
                return 0 if DISABLE_ILV else 32 * (m // 4)

            for m in range(NT):
                vector.wait_ge(b_lda, 16 * (m + 1))
                vector.memset(s_bmid[:, :], HI0 / 2)
                vector.memset(s_bstp[:, :], HI0 / 4)
                vector.drain()
                vector.engine_nop().then_inc(b_mid, 1)
                state["nbmid"] += 1
                for it in range(N_IT):
                    if it % 2 == 0:
                        vector.tensor_scalar(s_btr[:, :], s_ba[:, 0:IH],
                                             s_bmid[:, 0:1], 0.0,
                                             ALU.is_le, ALU.add,
                                             accum_out=s_bcnt[:, :])
                        vector.drain()
                        vector.tensor_scalar(s_btr[:, :], s_ba[:, IH:I],
                                             s_bmid[:, 0:1], 0.0,
                                             ALU.is_le, ALU.add,
                                             accum_out=s_st0[:, 14:15])
                        vector.drain()
                        vector.tensor_tensor(s_bcnt[:, :], s_bcnt[:, :],
                                             s_st0[:, 14:15], ALU.add)
                        vector.drain()
                    else:
                        state["nbcnt"] += 1
                        vector.wait_ge(b_cnt, state["nbcnt"])
                        vector.tensor_tensor(s_bcnt[:, :], s_bcnt[:, :],
                                             s_st0[:, 13:14], ALU.add)
                        vector.drain()
                        vector.tensor_scalar(s_bcnt[:, :], s_bcnt[:, :], 0.5,
                                             float(I) / 2, ALU.mult, ALU.add)
                        vector.drain()
                    vector.tensor_scalar(s_bsel[:, :], s_bcnt[:, :], float(K_TOP),
                                         None, ALU.is_ge)
                    vector.drain()
                    vector.tensor_scalar(s_bsel[:, :], s_bsel[:, :], -2.0, 1.0,
                                         ALU.mult, ALU.add)
                    vector.drain()
                    vector.scalar_tensor_tensor(s_bmid[:, :], s_bsel[:, :],
                                                s_bstp[:, 0:1], s_bmid[:, :],
                                                ALU.mult, ALU.add)
                    vector.tensor_scalar(s_bstp[:, :], s_bstp[:, :], 0.5, None,
                                         ALU.mult)
                    vector.drain()
                    if it < N_IT - 1:
                        vector.engine_nop().then_inc(b_mid, 1)
                        state["nbmid"] += 1
                    if it % 2 == 1 and state["piece"] < pieces_unlocked(m):
                        emit_evac()
                for q in range(NPQ):
                    pidx = NPQ * m + q
                    vector.wait_ge(b_lp0 if pidx % 2 == 0 else b_lp1,
                                   16 * (pidx // 2 + 1))
                    if pidx >= 2:
                        sp = (pidx - 2) % 2
                        vector.wait_ge(b_st0 if sp == 0 else b_st1,
                                       16 * ((pidx - 2) // 2 + 1))
                    vector.scalar_tensor_tensor(
                        s_bsr[:, pidx % 2, :],
                        s_ba[:, q * IQ:(q + 1) * IQ],
                        s_bmid[:, 0:1],
                        s_bpr[:, pidx % 2, :],
                        ALU.is_gt, ALU.mult).then_inc(b_smr, 1)
                    vector.drain()
                    state["nsmr"] += 1
            while state["piece"] < NPIECE:
                emit_evac()

        @block.scalar
        def _(scalar):
            nbmid = 0
            for m in range(NT):
                if m == 0:
                    scalar.wait_ge(a_ast, 16 * N_GR * 8)
                    scalar.wait_ge(a_pst, 16 * N_UR * 8)
                else:
                    scalar.wait_ge(b_smr, NPQ * m)
                scalar.dma_start(s_ba[:, :],
                                 d_a[m * 128:(m + 1) * 128, :]).then_inc(b_lda, 16)
                nbmid += 1
                for it in range(N_IT):
                    if it % 2 == 1:
                        scalar.wait_ge(b_mid, nbmid)
                        scalar.activation(s_btr[:, :], s_ba[:, 0:IH], AF.Sign,
                                          bias=s_bmid[:, :], scale=-1.0,
                                          accum_out=s_bcnt[:, :])
                        scalar.drain()
                        scalar.activation(s_btr[:, :], s_ba[:, IH:I], AF.Sign,
                                          bias=s_bmid[:, :], scale=-1.0,
                                          accum_out=s_st0[:, 13:14])
                        scalar.drain()
                        scalar.nop().then_inc(b_cnt, 1)
                    if it < N_IT - 1:
                        nbmid += 1
                for q in range(NPQ):
                    pidx = NPQ * m + q
                    if q >= 2:
                        scalar.wait_ge(b_smr, pidx - 1)
                        sp = (pidx - 2) % 2
                        scalar.dma_start(
                            d_sm[m * 128:(m + 1) * 128,
                                 (q - 2) * IQ:(q - 1) * IQ],
                            s_bsr[:, sp, :]).then_inc(
                                b_st0 if sp == 0 else b_st1, 16)
                    if pidx >= 2:
                        scalar.wait_ge(b_smr, pidx - 1)
                    scalar.dma_start(
                        s_bpr[:, pidx % 2, :],
                        d_p[m * 128:(m + 1) * 128,
                            q * IQ:(q + 1) * IQ]).then_inc(
                            b_lp0 if pidx % 2 == 0 else b_lp1, 16)
                for q in (NPQ - 2, NPQ - 1):
                    pidx = NPQ * m + q
                    scalar.wait_ge(b_smr, pidx + 1)
                    sp = pidx % 2
                    scalar.dma_start(
                        d_sm[m * 128:(m + 1) * 128, q * IQ:(q + 1) * IQ],
                        s_bsr[:, sp, :]).then_inc(
                            b_st0 if sp == 0 else b_st1, 16)

        @block.gpsimd
        def _(gpsimd):
            def xg_load(p):
                gidx = p // 32
                he = (p // 4) % NHE
                ts_ = p % 4
                m = gidx * 4 + ts_
                if p >= 2:
                    gpsimd.wait_ge(c_fin, p - 1)
                sx = c_xg0 if p % 2 == 0 else c_xg1
                gpsimd.dma_start(
                    s_xpc[:, p % 2, :],
                    t_x[m * 128:(m + 1) * 128,
                        he * 512:(he + 1) * 512]).then_inc(sx, 16)
                gpsimd.dma_start(
                    s_gpc[:, p % 2, :],
                    d_g[m * 128:(m + 1) * 128,
                        he * 512:(he + 1) * 512]).then_inc(sx, 16)
            xg_load(0)
            xg_load(1)
            for p in range(NPIECE):
                gidx = p // 32
                he = (p // 4) % NHE
                ts_ = p % 4
                m = gidx * 4 + ts_
                gpsimd.wait_ge(c_fin, p + 1)
                gpsimd.dma_start(
                    t_out[m * 128:(m + 1) * 128, he * 512:(he + 1) * 512],
                    s_outc[:, p % 3, :]).then_inc(
                        [c_os0, c_os1, c_os2][p % 3], 16)
                if p + 2 < NPIECE:
                    xg_load(p + 2)
            gpsimd.wait_ge(c_os0, 16 * ((NPIECE + 2) // 3))
            gpsimd.wait_ge(c_os1, 16 * ((NPIECE + 1) // 3))
            gpsimd.wait_ge(c_os2, 16 * (NPIECE // 3))

    return nc


def kernel(L0_attn_x, L0_attn_w_gate, L0_attn_w_up, L0_attn_w_down,
           L0_ffn_W_res, L0_ffn_W_pre, L0_ffn_W_post, L0_ffn_H_res_base,
           L0_ffn_H_pre_base, L0_ffn_H_post_base, L0_ffn_gate, L0_ffn_W_branch):
    x = np.ascontiguousarray(np.asarray(L0_attn_x, np.float32).reshape(NCORES * T, H))
    wgT = np.ascontiguousarray(np.asarray(L0_attn_w_gate, np.float32).T)
    wuT = np.ascontiguousarray(np.asarray(L0_attn_w_up, np.float32).T)
    wdT = np.ascontiguousarray(np.asarray(L0_attn_w_down, np.float32).T)
    wcT = np.ascontiguousarray(
        np.concatenate([np.asarray(L0_ffn_W_res, np.float32),
                        np.asarray(L0_ffn_W_pre, np.float32),
                        np.asarray(L0_ffn_W_post, np.float32)], 0).T)
    wbr = np.ascontiguousarray(np.asarray(L0_ffn_W_branch, np.float32))
    bases = np.ascontiguousarray(np.tile(
        np.concatenate([np.asarray(L0_ffn_H_res_base, np.float32).reshape(16),
                        np.asarray(L0_ffn_H_pre_base, np.float32).reshape(4),
                        np.asarray(L0_ffn_H_post_base, np.float32).reshape(4)]),
        (128, 1)))
    eyeb = np.ascontiguousarray(np.tile(np.eye(4, dtype=np.float32).reshape(16),
                                        (128, 1)))
    onesb = np.ones((1, 128), dtype=bf16)
    gate = np.ascontiguousarray(np.asarray(L0_ffn_gate, np.float32).reshape(1, 1))

    nc = build_program()
    in_maps = []
    for ci in range(NCORES):
        in_maps.append({
            "x": np.ascontiguousarray(x[ci * T:(ci + 1) * T]),
            "wgT": wgT, "wuT": wuT, "wdT": wdT, "wcT": wcT, "wbr": wbr,
            "bases": bases, "eyeb": eyeb, "onesb": onesb, "gatev": gate,
        })
    res = run_bass_kernel_spmd(nc, in_maps, core_ids=list(range(NCORES)))
    out = np.concatenate([np.asarray(r["out"], np.float32) for r in res.results], 0)
    return out.reshape(4, 4096, 4096)


# version 24 (final)
# speedup vs baseline: 1.0980x; 1.0980x over previous
import numpy as np
import ml_dtypes
import concourse.bass as bass
import concourse.mybir as mybir
from concourse.bass_utils import run_bass_kernel_spmd

F32 = mybir.dt.float32
BF16 = mybir.dt.bfloat16
U8 = mybir.dt.uint8
AF = mybir.ActivationFunctionType
ALU = mybir.AluOpType
AX = mybir.AxisListType
bf16 = ml_dtypes.bfloat16

NCORES = 8
T = 2048
NT = 16
H = 4096
KH = 32
I = 11008
K_TOP = I // 2
CHW = [512] * 21 + [256]
NCH = len(CHW)
CHOFF = [sum(CHW[:i]) for i in range(NCH)]
IK = 86
NHE = 8
NG = 4
C_IKB = [(7 * i, 7) for i in range(12)] + [(84, 2)]
NBLK_HE = len(C_IKB)
N_IT = 16
HI0 = 2.0
LN_EPS = 1e-5
RMS_EPS = 1.1920929e-07
SINK_IT = 50
TAU_INV = 10.0
INV_H = 1.0 / H

IQ = I // 16
NPQ = 16
IHB = I // 2

ROUNDS = []
for _c in range(NCH):
    for _mat, _grp in (("g", 0), ("u", 0), ("g", 1), ("u", 1)):
        ROUNDS.append((_c, _mat, _grp))
NR = len(ROUNDS)
G_IDX = {}
U_IDX = {}
_g = _u = 0
for _r, (_c, _mat, _grp) in enumerate(ROUNDS):
    if _mat == "g":
        G_IDX[_r] = _g; _g += 1
    else:
        U_IDX[_r] = _u; _u += 1
N_GR, N_UR = _g, _u

R_D1 = 8
Y_SLOTS = {24 + 4 * i: (2 * i, 2 * i + 2) for i in range(8)}

W_WC = 16
W_WB = 32
W_CHUNK = lambda c: 32 + 16 * 2 * (c + 1)
W_WD = lambda ik_end: 16 * (46 + ik_end)


def evac_wait(tensor, r, bk, a_slr, a_pr):
    if ROUNDS[r][1] == "g":
        tensor.wait_ge(a_slr, 8 * G_IDX[r] + bk + 1)
    else:
        tensor.wait_ge(a_pr, 8 * U_IDX[r] + bk + 1)


def build_program(detect_race_conditions=True, debug=()):
    nc = bass.Bass("TRN2", target_bir_lowering=False,
                   detect_race_conditions=detect_race_conditions)
    dbg = lambda n: "ExternalOutput" if n in debug else "Internal"

    t_x = nc.dram_tensor("x", [T, H], F32, kind="ExternalInput")
    t_wgT = nc.dram_tensor("wgT", [H, I], F32, kind="ExternalInput")
    t_wuT = nc.dram_tensor("wuT", [H, I], F32, kind="ExternalInput")
    t_wdT = nc.dram_tensor("wdT", [I, H], F32, kind="ExternalInput")
    t_wcT = nc.dram_tensor("wcT", [H, 24], F32, kind="ExternalInput")
    t_wbr = nc.dram_tensor("wbr", [1024, 1024], F32, kind="ExternalInput")
    t_bases = nc.dram_tensor("bases", [128, 24], F32, kind="ExternalInput")
    t_eyeb = nc.dram_tensor("eyeb", [128, 16], F32, kind="ExternalInput")
    t_ones = nc.dram_tensor("onesb", [1, 128], BF16, kind="ExternalInput")
    t_gate = nc.dram_tensor("gatev", [1, 1], F32, kind="ExternalInput")
    t_out = nc.dram_tensor("out", [T, H], F32, kind="ExternalOutput")

    d_xn = nc.dram_tensor("d_xn", [T, H], BF16, kind=dbg("d_xn"))
    d_wg = nc.dram_tensor("d_wg", [NCH, 4, 8, 128, 512], BF16, kind="Internal")
    d_wu = nc.dram_tensor("d_wu", [NCH, 4, 8, 128, 512], BF16, kind="Internal")
    d_wd = nc.dram_tensor("d_wd", [NHE, IK, 128, 512], BF16, kind="Internal")
    d_wc = nc.dram_tensor("d_wc", [H, 24], BF16, kind="Internal")
    d_wb = nc.dram_tensor("d_wb", [1024, 1024], BF16, kind="Internal")
    d_a = nc.dram_tensor("d_a", [T, I], F32, kind=dbg("d_a"))
    d_p = nc.dram_tensor("d_p", [T, I], BF16, kind=dbg("d_p"))
    d_sm = nc.dram_tensor("d_sm", [T, I], BF16, kind=dbg("d_sm"))
    d_xpre = nc.dram_tensor("d_xpre", [T, 1024], BF16, kind=dbg("d_xpre"))
    d_y = nc.dram_tensor("d_y", [T, 1024], BF16, kind=dbg("d_y"))
    d_g = nc.dram_tensor("d_g", [T, H], BF16, kind=dbg("d_g"))

    ARENA_SZ = 212832
    arena = nc.sbuf_tensor("arena", [128, ARENA_SZ], U8).__enter__()
    arena_base = nc.lookup_mloc(arena).addr

    def at(name, shape, dtype, addr):
        from functools import reduce
        per = reduce(lambda a, b: a * b, shape[1:], 1) * mybir.dt.size(dtype)
        assert addr % 32 == 0 and addr + per <= ARENA_SZ, (name, addr, per)
        hnd = nc.virtual_sbuf_tensor(name, shape, dtype)
        nc._tensor(hnd.name, hnd.shape, hnd.dtype, type="SB")
        mloc = nc.lookup_mloc(hnd)
        mloc.allocated = True
        mloc.addr = arena_base + addr
        return hnd

    s_rrms = at("s_rrms", [128, NT], F32, 0)
    s_st0 = at("s_st0", [128, 16], F32, 64)
    s_lnst = at("s_lnst", [128, NT, 8], F32, 128)
    s_bases = at("s_bases", [128, 24], F32, 640)
    s_eyeb = at("s_eyeb", [128, 16], F32, 736)
    s_ones = at("s_ones", [1, 128], BF16, 800)
    s_wcat = at("s_wcat", [128, KH, 24], BF16, 1056)
    s_Hpre = at("s_Hpre", [128, NT * 4], F32, 2592)
    s_Hpost = at("s_Hpost", [128, NT * 4], F32, 2848)
    s_rr1 = at("s_rr1", [128, NT], F32, 3104)
    s_rr2 = at("s_rr2", [128, NT], F32, 3168)
    s_zmax = at("s_zmax", [128, NT], F32, 3232)
    s_sa = at("s_sa", [128, NT * 4], F32, 3296)
    s_sb = at("s_sb", [128, NT * 4], F32, 3552)
    s_ssum = at("s_ssum", [128, NT * 4], F32, 3808)
    s_g128 = at("s_g128", [128, 1], F32, 4064)
    s_omg = at("s_omg", [128, 1], F32, 4096)
    s_gtin = at("s_gtin", [1, 1], F32, 4128)
    s_gex = at("s_gex", [1, 2], F32, 4160)
    s_gbf = at("s_gbf", [1, 1], BF16, 4192)
    s_bmid = at("s_bmid", [128, 2], F32, 4224)
    s_bstp = at("s_bstp", [128, 2], F32, 4256)
    s_bcnt = at("s_bcnt", [128, 2], F32, 4288)
    s_bsel = at("s_bsel", [128, 2], F32, 4320)

    XB = 4352
    s_x0 = at("s_x0", [128, 2, H], F32, XB)
    s_Z = at("s_Z", [128, NT * 16], F32, XB)
    s_T1 = at("s_T1", [128, NT * 16], F32, XB + 1024)
    s_E = at("s_E", [128, NT * 16], F32, XB + 2048)
    s_u4 = at("s_u4", [128, NT * 16], F32, XB + 3072)
    s_v4 = at("s_v4", [128, NT * 16], F32, XB + 4096)
    s_logit = at("s_logit", [128, NT, 24], F32, XB + 5120)
    s_ym = at("s_ym", [128, 2, 1024], BF16, XB + 5120)
    s_xpm = at("s_xpm", [128, 1024], BF16, XB + 9216)
    s_xpT = at("s_xpT", [128, 2, 8, 128], BF16, XB + 11264)
    s_wbrs = at("s_wbrs", [128, 8, 1024], BF16, XB + 15360)
    s_fm = at("s_fm", [128, 2, 1024], F32, XB + 15360)
    s_gm = at("s_gm", [128, 2, 512], BF16, XB + 23552)
    s_Hres = at("s_Hres", [128, NT * 16], F32, XB + 25600)
    s_btrv = at("s_btrv", [128, IHB], BF16, XB + 4096)
    s_btrs = at("s_btrs", [128, IHB], BF16, XB + 15360)
    X_END = XB + 32768

    A1 = X_END
    s_xnT = at("s_xnT", [128, KH, T], BF16, A1)
    s_smT = at("s_smT", [128, IK, 512], BF16, A1)
    s_wdb = at("s_wdb", [128, 2, 7, 512], BF16, A1 + 88064)
    s_outc = at("s_outc", [128, 2, 512], F32, A1 + 102400)
    s_xpc = at("s_xpc", [128, 2, 512], F32, A1 + 106496)
    s_gpc = at("s_gpc", [128, 2, 512], BF16, A1 + 110592)
    s_bpr = at("s_bpr", [128, 2, IQ], BF16, A1 + 112640)
    s_bsr = at("s_bsr", [128, 2, IQ], BF16, A1 + 115424)
    A1_END = A1 + 131072

    A2 = A1_END
    s_strip = at("s_strip", [128, 2, 8, 512], BF16, A2)
    s_sq = at("s_sq", [128, H], BF16, A2)
    s_sl = at("s_sl", [128, 8, 512], F32, A2 + 16384)
    s_xnb = at("s_xnb", [128, 2, H], BF16, A2 + 16384)
    s_ppc = at("s_ppc", [128, 2, 512], BF16, A2 + 32768)
    s_ba = at("s_ba", [128, I], F32, A2)
    s_xnm = at("s_xnm", [128, H], BF16, A2 + 34816)
    assert A2 + 34816 + 8192 <= ARENA_SZ
    assert A2 + 44032 <= ARENA_SZ

    sems = []
    def sem(n):
        s = nc.semaphore(n).__enter__()
        sems.append(s)
        return s

    ln_x0 = sem("ln_x0"); ln_x1 = sem("ln_x1"); ln_v = sem("ln_v")
    ln_r = sem("ln_r"); ln_vn = sem("ln_vn"); ln_st = sem("ln_st")
    x_t = sem("x_t"); w_cast = sem("w_cast"); k_cst = sem("k_cst")
    d_wl = sem("d_wl")
    a_strip = sem("a_strip"); a_mm = sem("a_mm"); a_kb = sem("a_kb")
    a_slr = sem("a_slr"); a_ar = sem("a_ar"); a_ast = sem("a_ast")
    a_pr = sem("a_pr"); a_pst = sem("a_pst")
    d1_mm = sem("d1_mm"); d1_ev = sem("d1_ev")
    z_rdy = sem("z_rdy"); ez_rdy = sem("ez_rdy")
    gs1 = sem("gs1"); gs2 = sem("gs2"); gs_mm = sem("gs_mm"); gs_ev = sem("gs_ev")
    sx_rdy = sem("sx_rdy"); hp_exp = sem("hp_exp")
    xp_v = sem("xp_v"); xp_st = sem("xp_st"); xp_tl = sem("xp_tl")
    ymm = sem("ymm"); y_ev = sem("y_ev"); y_st = sem("y_st"); ym_ld = sem("ym_ld")
    xn_ld = sem("xn_ld"); d5_v = sem("d5_v"); g_st = sem("g_st")
    b_lda = sem("b_lda"); b_mid = sem("b_mid"); b_cnt = sem("b_cnt")
    b_smr = sem("b_smr"); b_st0 = sem("b_st0"); b_st1 = sem("b_st1")
    b_lp0 = sem("b_lp0"); b_lp1 = sem("b_lp1")
    c_smld = sem("c_smld"); c_wd = sem("c_wd"); c_blk = sem("c_blk")
    c_ev = sem("c_ev"); c_fin = sem("c_fin")
    c_xg0 = sem("c_xg0"); c_xg1 = sem("c_xg1")
    c_os0 = sem("c_os0"); c_os1 = sem("c_os1")
    g_ld = sem("g_ld")

    pb = [nc.psum_tensor(f"pb{b}", [128, 512], F32).__enter__() for b in range(8)]

    NPIECE = NG * NHE * 4

    def piece_tile(p):
        return 4 * (p // 32 + 1) + (p % 32) // 8

    with nc.Block() as block:

        @block.gpsimd
        def _(gpsimd):
            gpsimd.dma_start(d_wc[:, :], t_wcT[:, :]).then_inc(w_cast, 16)
            gpsimd.wait_ge(w_cast, 16)
            gpsimd.dma_start(d_wb[:, :], t_wbr[:, :]).then_inc(w_cast, 16)
            gpsimd.wait_ge(w_cast, 32)
            for c in range(NCH):
                W = CHW[c]
                for tW, dW in ((t_wgT, d_wg), (t_wuT, d_wu)):
                    gpsimd.dma_start(
                        dW[c, :, :, :, 0:W].rearrange("kb k p w -> (kb k p) w"),
                        tW[:, CHOFF[c]:CHOFF[c] + W]).then_inc(w_cast, 16)
                gpsimd.wait_ge(w_cast, 32 + 16 * 2 * (c + 1))
            for ik in range(IK):
                gpsimd.dma_start(
                    d_wd[:, ik, :, :],
                    t_wdT[ik * 128:(ik + 1) * 128, :].rearrange(
                        "p (he w) -> he p w", he=NHE)).then_inc(w_cast, 16)
                if ik % 8 == 7:
                    gpsimd.wait_ge(w_cast, W_WD(ik + 1))

            gpsimd.wait_ge(a_mm, 8 * NR)
            gpsimd.wait_ge(a_ast, 16 * 8 * N_GR)
            gpsimd.wait_ge(a_pst, 16 * 8 * N_UR)

            def xg_load(p):
                gidx = p // 32
                he = (p // 4) % NHE
                ts_ = p % 4
                m = gidx * 4 + ts_
                if p >= 2:
                    gpsimd.wait_ge(c_fin, p - 1)
                sx = c_xg0 if p % 2 == 0 else c_xg1
                gpsimd.wait_ge(g_st, 16 * 8 * (m + 1))
                gpsimd.dma_start(
                    s_xpc[:, p % 2, :],
                    t_x[m * 128:(m + 1) * 128,
                        he * 512:(he + 1) * 512]).then_inc(sx, 16)
                gpsimd.dma_start(
                    s_gpc[:, p % 2, :],
                    d_g[m * 128:(m + 1) * 128,
                        he * 512:(he + 1) * 512]).then_inc(sx, 16)

            def out_store(p):
                gidx = p // 32
                he = (p // 4) % NHE
                ts_ = p % 4
                m = gidx * 4 + ts_
                gpsimd.wait_ge(c_fin, p + 1)
                gpsimd.dma_start(
                    t_out[m * 128:(m + 1) * 128, he * 512:(he + 1) * 512],
                    s_outc[:, p % 2, :]).then_inc(
                        c_os0 if p % 2 == 0 else c_os1, 16)

            ld_p = 0
            st_p = 0
            for m in range(NT):
                if m >= 1:
                    gpsimd.wait_ge(b_smr, NPQ * m)
                gpsimd.dma_start(s_ba[:, :],
                                 d_a[m * 128:(m + 1) * 128, :]).then_inc(b_lda, 16)
                while st_p < NPIECE and piece_tile(st_p) <= m:
                    if ld_p < st_p + 2 and ld_p < NPIECE:
                        xg_load(ld_p); ld_p += 1
                        if ld_p < st_p + 2 and ld_p < NPIECE:
                            xg_load(ld_p); ld_p += 1
                    out_store(st_p); st_p += 1
            while st_p < NPIECE:
                while ld_p < st_p + 2 and ld_p < NPIECE:
                    xg_load(ld_p); ld_p += 1
                out_store(st_p); st_p += 1
            gpsimd.wait_ge(c_os0, 16 * (NPIECE // 2))
            gpsimd.wait_ge(c_os1, 16 * (NPIECE // 2))

        @block.sync
        def _(sync):
            sync.dma_start(s_bases[:, :], t_bases[:, :]).then_inc(k_cst, 16)
            sync.dma_start(s_eyeb[:, :], t_eyeb[:, :]).then_inc(k_cst, 16)
            sync.dma_start(s_ones[:, :], t_ones[:, :]).then_inc(k_cst, 16)
            for m in range(0, NT, 2):
                if m >= 2:
                    sync.wait_ge(ln_vn, m - 1)
                sync.dma_start(s_x0[:, m % 2, :],
                               t_x[m * 128:(m + 1) * 128, :]).then_inc(ln_x0, 16)
            sync.wait_ge(w_cast, W_WC)
            sync.dma_start(s_wcat[:, :, :],
                           d_wc[:, :].rearrange("(k p) c -> p k c", p=128)
                           ).then_inc(d_wl, 16)
            for h in range(2):
                sync.wait_ge(ln_st, 16 * 8 * (h + 1))
                for k in range(KH):
                    sync.dma_start(
                        s_xnT[:, k, h * 1024:(h + 1) * 1024],
                        d_xn[h * 1024:(h + 1) * 1024, k * 128:(k + 1) * 128],
                        transpose=True).then_inc(x_t, 16)
            sync.wait_ge(ln_vn, NT)
            blk_idx = 0
            wbrs_done = False
            xn1 = 0
            dd5 = 0
            for r, (c, mat, grp) in enumerate(ROUNDS):
                W = CHW[c]
                dW = d_wg if mat == "g" else d_wu
                for kb in range(4):
                    if blk_idx >= 2:
                        sync.wait_ge(a_kb, blk_idx - 1)
                    sync.wait_ge(w_cast, W_CHUNK(c))
                    sync.dma_start(
                        s_strip[:, blk_idx % 2, :, 0:W],
                        dW[c, kb, :, :, 0:W].rearrange("k p w -> p k w"),
                    ).then_inc(a_strip, 16)
                    blk_idx += 1
                if r == 9 and not wbrs_done:
                    sync.wait_ge(w_cast, W_WB)
                    sync.dma_start(s_wbrs[:, :, :],
                                   d_wb[:, :].rearrange("(k p) c -> p k c", p=128)
                                   ).then_inc(d_wl, 16)
                    wbrs_done = True
                if 9 <= r <= 26 and xn1 < NT:
                    m = xn1
                    if m >= 1:
                        sync.wait_ge(xp_v, m)
                    sync.dma_start(s_xnm[:, :],
                                   d_xn[m * 128:(m + 1) * 128, :]).then_inc(xn_ld, 16)
                    xn1 += 1
                for R, (m0, m1) in Y_SLOTS.items():
                    if r == R - 3:
                        for m in range(m0, m1):
                            sync.wait_ge(xp_st, 16 * (m + 1))
                            if m >= 2:
                                sync.wait_ge(ymm, 2 * (m - 1))
                            for ck in range(8):
                                sync.dma_start(
                                    s_xpT[:, m % 2, ck, :],
                                    d_xpre[m * 128:(m + 1) * 128,
                                           ck * 128:(ck + 1) * 128],
                                    transpose=True).then_inc(xp_tl, 16)
                if r >= 53 and dd5 < NT:
                    m = dd5
                    if m == 0:
                        sync.wait_ge(xp_v, NT)
                        sync.wait_ge(y_st, 16 * NT)
                    else:
                        sync.wait_ge(d5_v, 4 * m)
                    sync.dma_start(s_xnm[:, :],
                                   d_xn[m * 128:(m + 1) * 128, :]).then_inc(xn_ld, 16)
                    sync.dma_start(s_ym[:, m % 2, :],
                                   d_y[m * 128:(m + 1) * 128, :]).then_inc(ym_ld, 16)
                    dd5 += 1
            while dd5 < NT:
                m = dd5
                if m == 0:
                    sync.wait_ge(xp_v, NT)
                    sync.wait_ge(y_st, 16 * NT)
                else:
                    sync.wait_ge(d5_v, 4 * m)
                sync.dma_start(s_xnm[:, :],
                               d_xn[m * 128:(m + 1) * 128, :]).then_inc(xn_ld, 16)
                sync.dma_start(s_ym[:, m % 2, :],
                               d_y[m * 128:(m + 1) * 128, :]).then_inc(ym_ld, 16)
                dd5 += 1

            blkj = 0
            for gidx in range(NG):
                sync.wait_ge(b_st0, 16 * 32 * (gidx + 1))
                sync.wait_ge(b_st1, 16 * 32 * (gidx + 1))
                if gidx >= 1:
                    sync.wait_ge(c_blk, NBLK_HE * NHE * gidx)
                for ik in range(IK):
                    sync.dma_start(
                        s_smT[:, ik, :],
                        d_sm[gidx * 512:(gidx + 1) * 512,
                             ik * 128:(ik + 1) * 128],
                        transpose=True).then_inc(c_smld, 16)
                for he in range(NHE):
                    for bi, (ik0, nik) in enumerate(C_IKB):
                        if blkj >= 2:
                            sync.wait_ge(c_blk, blkj - 1)
                        sync.wait_ge(w_cast, W_WD(ik0 + nik))
                        sync.dma_start(
                            s_wdb[:, blkj % 2, 0:nik, :],
                            d_wd[he, ik0:ik0 + nik, :, :].rearrange(
                                "ik p w -> p ik w")).then_inc(c_wd, 16)
                        blkj += 1

        @block.scalar
        def _(scalar):
            scalar.dma_start(s_gtin[:, :], t_gate[:, :]).then_inc(g_ld, 16)
            scalar.dma_start(s_x0[:, 1, :], t_x[128:256, :]).then_inc(ln_x1, 16)
            scalar.wait_ge(g_ld, 16)
            scalar.activation(s_gex[:, 0:1], s_gtin[:, :], AF.Exp, scale=-1.0)
            scalar.drain()
            scalar.nop().then_inc(gs1, 1)
            for m in range(NT):
                scalar.wait_ge(ln_v, m + 1)
                scalar.activation(s_lnst[:, m, 6:7], s_lnst[:, m, 5:6], AF.Ln)
                scalar.drain()
                scalar.activation(s_lnst[:, m, 6:7], s_lnst[:, m, 6:7],
                                  AF.Exp, scale=-0.5)
                scalar.drain()
                scalar.activation(s_lnst[:, m, 7:8], s_lnst[:, m, 4:5], AF.Ln)
                scalar.drain()
                scalar.activation(s_rrms[:, m:m + 1], s_lnst[:, m, 7:8],
                                  AF.Exp, scale=-0.5)
                scalar.drain()
                scalar.nop().then_inc(ln_r, 1)
                scalar.wait_ge(ln_vn, m + 1)
                scalar.dma_start(d_xn[m * 128:(m + 1) * 128, :],
                                 s_xnb[:, m % 2, :]).then_inc(ln_st, 16)
                if m % 2 == 1 and m + 2 < NT:
                    scalar.dma_start(
                        s_x0[:, (m + 2) % 2, :],
                        t_x[(m + 2) * 128:(m + 3) * 128, :]).then_inc(ln_x1, 16)

            apc_i = ppc_i = 0
            xp_done = 0
            d5s = 0
            st = {"ez": False, "hp": False}

            def d5_chunk(q):
                m, cq = q // 8, q % 8
                e = 4 * m + cq // 2
                scalar.wait_ge(d5_v, e + 1)
                if q >= 2:
                    scalar.wait_ge(g_st, 16 * (q - 1))
                scalar.activation(s_gm[:, q % 2, :],
                                  s_fm[:, e % 2,
                                       (cq % 2) * 512:((cq % 2) + 1) * 512],
                                  AF.Silu)
                scalar.drain()
                scalar.dma_start(
                    d_g[m * 128:(m + 1) * 128,
                        (cq // 2) * 1024 + (cq % 2) * 512:
                        (cq // 2) * 1024 + (cq % 2) * 512 + 512],
                    s_gm[:, q % 2, :]).then_inc(g_st, 16)

            def aux_scalar(r):
                nonlocal xp_done, d5s
                if r == 9 and not st["ez"]:
                    scalar.wait_ge(z_rdy, 1)
                    scalar.activation(s_E[:, :], s_Z[:, :], AF.Exp)
                    scalar.drain()
                    scalar.nop().then_inc(ez_rdy, 1)
                    st["ez"] = True
                if r == 9 and not st["hp"]:
                    scalar.wait_ge(sx_rdy, 1)
                    scalar.activation(s_Hpre[:, :], s_Hpre[:, :], AF.Exp)
                    scalar.drain()
                    scalar.activation(s_Hpost[:, :], s_Hpost[:, :], AF.Exp)
                    scalar.drain()
                    scalar.nop().then_inc(hp_exp, 1)
                    st["hp"] = True
                while xp_done < NT and r >= 10 and xp_done <= (r - 10):
                    m = xp_done
                    scalar.wait_ge(xp_v, m + 1)
                    scalar.dma_start(d_xpre[m * 128:(m + 1) * 128, :],
                                     s_xpm[:, :]).then_inc(xp_st, 16)
                    xp_done += 1
                for R, (m0, m1) in Y_SLOTS.items():
                    if r == R + 1:
                        for m in range(m0, m1):
                            scalar.wait_ge(y_ev, m + 1)
                            scalar.dma_start(d_y[m * 128:(m + 1) * 128, :],
                                             s_ym[:, m % 2, :]).then_inc(y_st, 16)
                if r >= 54:
                    target = min(8 * NT, 8 * (r - 53))
                    while d5s < target:
                        d5_chunk(d5s); d5s += 1

            for r, (c, mat, grp) in enumerate(ROUNDS):
                W = CHW[c]
                pair = 2 * c + grp
                for ts_ in range(8):
                    bank = ts_
                    t_glob = grp * 8 + ts_
                    if mat == "g":
                        scalar.wait_ge(a_mm, 8 * r + ts_ + 1)
                        if pair >= 1:
                            scalar.wait_ge(a_ast, 16 * (8 * (pair - 1) + ts_ + 1))
                        scalar.activation(s_sl[:, ts_, 0:W], pb[bank][:, 0:W],
                                          AF.Silu)
                        scalar.drain()
                        scalar.nop().then_inc(a_slr, 1)
                    else:
                        scalar.wait_ge(a_ar, apc_i + 1)
                        scalar.dma_start(
                            d_a[t_glob * 128:(t_glob + 1) * 128,
                                CHOFF[c]:CHOFF[c] + W],
                            s_sl[:, ts_, 0:W]).then_inc(a_ast, 16)
                        apc_i += 1
                        scalar.wait_ge(a_pr, ppc_i + 1)
                        scalar.dma_start(
                            d_p[t_glob * 128:(t_glob + 1) * 128,
                                CHOFF[c]:CHOFF[c] + W],
                            s_ppc[:, ppc_i % 2, 0:W]).then_inc(a_pst, 16)
                        ppc_i += 1
                aux_scalar(r)
            while d5s < 8 * NT:
                d5_chunk(d5s); d5s += 1

            nbcnt = 0
            for m in range(NT):
                par = m % 2
                for it in range(N_IT):
                    if it % 2 == 1:
                        scalar.wait_ge(b_mid, 16 * m + 1 + it)
                        scalar.activation(s_btrs[:, :], s_ba[:, 0:IHB],
                                          AF.Sign, bias=s_bmid[:, par:par + 1],
                                          scale=-1.0,
                                          accum_out=s_st0[:, 8 + 2 * par:9 + 2 * par])
                        scalar.drain()
                        scalar.activation(s_btrs[:, :], s_ba[:, IHB:I],
                                          AF.Sign, bias=s_bmid[:, par:par + 1],
                                          scale=-1.0,
                                          accum_out=s_st0[:, 9 + 2 * par:10 + 2 * par])
                        scalar.drain()
                        scalar.nop().then_inc(b_cnt, 1)
                        nbcnt += 1
                for q in range(NPQ):
                    pidx = NPQ * m + q
                    if q >= 2:
                        scalar.wait_ge(b_smr, pidx - 1)
                        sp = (pidx - 2) % 2
                        scalar.dma_start(
                            d_sm[m * 128:(m + 1) * 128,
                                 (q - 2) * IQ:(q - 1) * IQ],
                            s_bsr[:, sp, :]).then_inc(
                                b_st0 if sp == 0 else b_st1, 16)
                    if pidx >= 2:
                        scalar.wait_ge(b_smr, pidx - 1)
                    scalar.dma_start(
                        s_bpr[:, pidx % 2, :],
                        d_p[m * 128:(m + 1) * 128,
                            q * IQ:(q + 1) * IQ]).then_inc(
                            b_lp0 if pidx % 2 == 0 else b_lp1, 16)
                for q in (NPQ - 2, NPQ - 1):
                    pidx = NPQ * m + q
                    scalar.wait_ge(b_smr, pidx + 1)
                    sp = pidx % 2
                    scalar.dma_start(
                        d_sm[m * 128:(m + 1) * 128, q * IQ:(q + 1) * IQ],
                        s_bsr[:, sp, :]).then_inc(
                            b_st0 if sp == 0 else b_st1, 16)

        @block.tensor
        def _(tensor):
            tensor.wait_ge(x_t, 16 * KH)
            blk_idx = 0
            kb3_pending = False
            for r, (c, mat, grp) in enumerate(ROUNDS):
                W = CHW[c]
                if r == R_D1:
                    tensor.wait_ge(d_wl, 16)
                    for m in range(NT):
                        bk = m % 2
                        if m < 2:
                            evac_wait(tensor, r - 1, bk, a_slr, a_pr)
                        else:
                            tensor.wait_ge(d1_ev, m - 1)
                        for k in range(KH):
                            mm = tensor.matmul(
                                pb[bk][:, 0:24],
                                s_xnT[:, k, m * 128:(m + 1) * 128],
                                s_wcat[:, k, :],
                                start=(k == 0), stop=(k == KH - 1))
                            if k == KH - 1:
                                mm.then_inc(d1_mm, 1)
                    evac_wait(tensor, r - 1, 2, a_slr, a_pr)
                    tensor.wait_ge(gs2, 1)
                    tensor.wait_ge(k_cst, 48)
                    tensor.matmul(pb[2][:, 0:1], s_ones[:, :], s_gbf[:, :],
                                  start=True, stop=True).then_inc(gs_mm, 1)
                if r in Y_SLOTS:
                    m0, m1 = Y_SLOTS[r]
                    tensor.wait_ge(d_wl, 32)
                    for m in range(m0, m1):
                        tensor.wait_ge(xp_tl, 16 * 8 * (m + 1))
                        for half in range(2):
                            bk = half
                            if m == m0:
                                evac_wait(tensor, r - 1, bk, a_slr, a_pr)
                            elif half == 0:
                                tensor.wait_ge(y_ev, m)
                            for ck in range(8):
                                mm = tensor.matmul(
                                    pb[bk][:, :],
                                    s_xpT[:, m % 2, ck, :],
                                    s_wbrs[:, ck, half * 512:(half + 1) * 512],
                                    start=(ck == 0), stop=(ck == 7))
                                if ck == 7:
                                    mm.then_inc(ymm, 1)
                for kb in range(4):
                    tensor.wait_ge(a_strip, 16 * (blk_idx + 1))
                    slot = blk_idx % 2
                    for ts_ in range(8):
                        bank = ts_
                        if kb == 0:
                            if r >= 1:
                                evac_wait(tensor, r - 1, ts_, a_slr, a_pr)
                            if r == R_D1 and ts_ < 2:
                                tensor.wait_ge(d1_ev, NT)
                            if r == R_D1 and ts_ == 2:
                                tensor.wait_ge(gs_ev, 1)
                            if r in Y_SLOTS and ts_ < 2:
                                tensor.wait_ge(y_ev, Y_SLOTS[r][1])
                        t_glob = grp * 8 + ts_
                        for kk in range(8):
                            k = kb * 8 + kk
                            mm = tensor.matmul(
                                pb[bank][:, 0:W],
                                s_xnT[:, k, t_glob * 128:(t_glob + 1) * 128],
                                s_strip[:, slot, kk, 0:W],
                                start=(kb == 0 and kk == 0),
                                stop=(kb == 3 and kk == 7))
                            if kb == 0 and ts_ == 0 and kk == 0 and kb3_pending:
                                mm.then_inc(a_kb, 1)
                                kb3_pending = False
                            if kb == 3 and kk == 7:
                                mm.then_inc(a_mm, 1)
                            if ts_ == 7 and kk == 7 and kb < 3:
                                mm.then_inc(a_kb, 1)
                    blk_idx += 1
                kb3_pending = True

            blkj = 0
            for gidx in range(NG):
                tensor.wait_ge(c_smld, 16 * IK * (gidx + 1))
                for he in range(NHE):
                    heg = gidx * NHE + he
                    bankbase = 0 if he % 2 == 0 else 4
                    for bi, (ik0, nik) in enumerate(C_IKB):
                        tensor.wait_ge(c_wd, 16 * (blkj + 1))
                        bslot = blkj % 2
                        for ts_ in range(4):
                            bank = bankbase + ts_
                            if bi == 0 and heg >= 2:
                                tensor.wait_ge(c_ev, 4 * (heg - 2) + ts_ + 1)
                            for ii in range(nik):
                                ik = ik0 + ii
                                mm = tensor.matmul(
                                    pb[bank][:, :],
                                    s_smT[:, ik, ts_ * 128:(ts_ + 1) * 128],
                                    s_wdb[:, bslot, ii, :],
                                    start=(bi == 0 and ii == 0),
                                    stop=(bi == NBLK_HE - 1 and ii == nik - 1))
                                if ts_ == 0 and ii == 0 and bi >= 1:
                                    mm.then_inc(c_blk, 1)
                                if (bi == NBLK_HE - 1 and ts_ == 3
                                        and ii == nik - 1):
                                    mm.then_inc(c_blk, 1)
                        blkj += 1

        @block.vector
        def _(vector):
            def ln_sums(m):
                xq = ln_x0 if m % 2 == 0 else ln_x1
                vector.wait_ge(xq, 16 * (m // 2 + 1))
                xa = s_x0[:, m % 2, :]
                vector.reduce_sum(s_lnst[:, m, 0:1], xa, AX.X)
                vector.tensor_tensor(s_sq[:, :], xa, xa, ALU.mult)
                vector.drain()
                vector.reduce_sum(s_lnst[:, m, 1:2], s_sq[:, :], AX.X)
                vector.drain()
                vector.tensor_scalar(s_lnst[:, m, 2:3], s_lnst[:, m, 0:1],
                                     INV_H, None, ALU.mult)
                vector.tensor_scalar(s_lnst[:, m, 3:4], s_lnst[:, m, 1:2],
                                     INV_H, None, ALU.mult)
                vector.drain()
                vector.tensor_tensor(s_lnst[:, m, 4:5], s_lnst[:, m, 2:3],
                                     s_lnst[:, m, 2:3], ALU.mult)
                vector.drain()
                vector.scalar_tensor_tensor(s_lnst[:, m, 5:6],
                                            s_lnst[:, m, 3:4], LN_EPS,
                                            s_lnst[:, m, 4:5],
                                            ALU.add, ALU.subtract)
                vector.drain()
                vector.tensor_scalar(s_lnst[:, m, 4:5], s_lnst[:, m, 3:4],
                                     RMS_EPS, None, ALU.add)
                vector.drain()
                vector.engine_nop().then_inc(ln_v, 1)

            def ln_norm(m):
                vector.wait_ge(ln_r, m + 1)
                if m >= 2:
                    vector.wait_ge(ln_st, 16 * (m - 1))
                vector.tensor_scalar(s_xnb[:, m % 2, :], s_x0[:, m % 2, :],
                                     s_lnst[:, m, 2:3], s_lnst[:, m, 6:7],
                                     ALU.subtract, ALU.mult)
                vector.drain()
                vector.engine_nop().then_inc(ln_vn, 1)

            ln_sums(0)
            for m in range(1, NT):
                ln_sums(m)
                ln_norm(m - 1)
            ln_norm(NT - 1)

            ZG = s_Z.rearrange("p (g ij) -> p g ij", ij=16)
            E4v = s_T1.rearrange("p (g i j) -> p g i j", i=4, j=4)
            u44 = s_u4.rearrange("p (g i j) -> p g i j", i=4, j=4)
            v44 = s_v4.rearrange("p (g i j) -> p g i j", i=4, j=4)
            saG = s_sa.rearrange("p (g i) -> p g i", i=4)
            sbG = s_sb.rearrange("p (g j) -> p g j", j=4)
            ssG = s_ssum.rearrange("p (g i) -> p g i", i=4)
            HpreG = s_Hpre.rearrange("p (g j) -> p g j", j=4)
            HpostG = s_Hpost.rearrange("p (g j) -> p g j", j=4)
            HresG = s_Hres.rearrange("p (g ij) -> p g ij", ij=16)
            HresQ = s_Hres.rearrange("p (g i j) -> p g i j", i=4, j=4)

            state = {"d1": False, "smax": False, "sinit": False, "sk": 0,
                     "hfin": False, "xpre": 0, "d5": 0}

            def av_d1z():
                for m in range(NT):
                    vector.wait_ge(d1_mm, m + 1)
                    vector.tensor_copy(s_logit[:, m, :], pb[m % 2][:, 0:24])
                    vector.drain()
                    vector.engine_nop().then_inc(d1_ev, 1)
                for m in range(NT):
                    vector.tensor_scalar(
                        s_Z[:, m * 16:(m + 1) * 16], s_logit[:, m, 0:16],
                        s_rrms[:, m:m + 1], TAU_INV, ALU.mult, ALU.mult)
                    vector.tensor_scalar(
                        s_Hpre[:, m * 4:(m + 1) * 4], s_logit[:, m, 16:20],
                        s_rrms[:, m:m + 1], None, ALU.mult)
                    vector.tensor_scalar(
                        s_Hpost[:, m * 4:(m + 1) * 4], s_logit[:, m, 20:24],
                        s_rrms[:, m:m + 1], None, ALU.mult)
                vector.drain()
                vector.wait_ge(k_cst, 48)
                vector.tensor_scalar(s_T1[:, 0:16], s_bases[:, 0:16],
                                     TAU_INV, None, ALU.mult)
                vector.drain()
                bview = s_T1[:, 0:16].unsqueeze(1).to_broadcast([128, NT, 16])
                vector.tensor_tensor(ZG, ZG, bview, ALU.add)
                pview = s_bases[:, 16:20].unsqueeze(1).to_broadcast([128, NT, 4])
                vector.tensor_tensor(HpreG, HpreG, pview, ALU.add)
                qview = s_bases[:, 20:24].unsqueeze(1).to_broadcast([128, NT, 4])
                vector.tensor_tensor(HpostG, HpostG, qview, ALU.add)
                vector.drain()
                vector.engine_nop().then_inc(sx_rdy, 1)
                vector.reduce_max(s_zmax[:, :], ZG, AX.X)
                vector.drain()
                mview = s_zmax[:, :].unsqueeze(2).to_broadcast([128, NT, 16])
                vector.tensor_tensor(ZG, ZG, mview, ALU.subtract)
                vector.drain()
                vector.tensor_scalar(s_Z[:, :], s_Z[:, :], -60.0, None, ALU.max)
                vector.drain()
                vector.engine_nop().then_inc(z_rdy, 1)

            def av_gs():
                vector.wait_ge(gs1, 1)
                vector.tensor_scalar(s_gex[:, 1:2], s_gex[:, 0:1], 1.0, None,
                                     ALU.add)
                vector.drain()
                vector.reciprocal(s_gtin[:, :], s_gex[:, 1:2])
                vector.drain()
                vector.tensor_copy(s_gbf[:, :], s_gtin[:, :])
                vector.drain()
                vector.engine_nop().then_inc(gs2, 1)

            def av_gsev():
                vector.wait_ge(gs_mm, 1)
                vector.tensor_copy(s_g128[:, :], pb[2][:, 0:1])
                vector.drain()
                vector.tensor_scalar(s_omg[:, :], s_g128[:, :], -1.0, 1.0,
                                     ALU.mult, ALU.add)
                vector.drain()
                vector.engine_nop().then_inc(gs_ev, 1)

            def av_smax():
                vector.wait_ge(hp_exp, 1)
                vector.reduce_sum(s_rr1[:, :], HpreG, AX.X)
                vector.reduce_sum(s_rr2[:, :], HpostG, AX.X)
                vector.drain()
                vector.reciprocal(s_rr1[:, :], s_rr1[:, :])
                vector.reciprocal(s_rr2[:, :], s_rr2[:, :])
                vector.drain()
                rpre = s_rr1[:, :].unsqueeze(2).to_broadcast([128, NT, 4])
                rpost = s_rr2[:, :].unsqueeze(2).to_broadcast([128, NT, 4])
                vector.tensor_tensor(HpreG, HpreG, rpre, ALU.mult)
                vector.tensor_tensor(HpostG, HpostG, rpost, ALU.mult)
                vector.drain()

            def av_sinit():
                vector.wait_ge(ez_rdy, 1)
                vector.memset(s_sb[:, :], 1.0)
                vector.drain()

            def av_sk_iter():
                for jj in range(4):
                    vector.tensor_copy(v44[:, :, jj, :], sbG[:, :, :])
                vector.drain()
                vector.tensor_tensor(s_T1[:, :], s_E[:, :], s_v4[:, :], ALU.mult)
                vector.drain()
                vector.reduce_sum(ssG[:, :, :], E4v[:, :, :, :], AX.X)
                vector.drain()
                vector.reciprocal(s_sa[:, :], s_ssum[:, :])
                vector.drain()
                for jj in range(4):
                    vector.tensor_copy(u44[:, :, :, jj], saG[:, :, :])
                vector.drain()
                vector.tensor_tensor(s_T1[:, :], s_E[:, :], s_u4[:, :], ALU.mult)
                vector.drain()
                vector.tensor_reduce(ssG[:, :, :],
                                     E4v[:, :, :, :].transpose([0, 1, 3, 2]),
                                     AX.X, ALU.add)
                vector.drain()
                vector.reciprocal(s_sb[:, :], s_ssum[:, :])
                vector.drain()

            def av_hfin():
                for jj in range(4):
                    vector.tensor_copy(v44[:, :, jj, :], sbG[:, :, :])
                vector.drain()
                vector.tensor_tensor(s_T1[:, :], s_E[:, :], s_u4[:, :], ALU.mult)
                vector.drain()
                vector.tensor_tensor(s_Hres[:, :], s_T1[:, :], s_v4[:, :],
                                     ALU.mult)
                vector.drain()
                vector.wait_ge(gs_ev, 1)
                vector.tensor_scalar(s_Hres[:, :], s_Hres[:, :],
                                     s_g128[:, 0:1], None, ALU.mult)
                vector.drain()
                eview = s_eyeb[:, :].unsqueeze(1).to_broadcast([128, NT, 16])
                vector.scalar_tensor_tensor(HresG, eview, s_omg[:, 0:1],
                                            HresG, ALU.mult, ALU.add)
                vector.drain()

            def av_xpre(m):
                vector.wait_ge(xn_ld, 16 * (m + 1))
                if m >= 1:
                    vector.wait_ge(xp_st, 16 * m)
                vector.tensor_scalar(s_xpm[:, :], s_xnm[:, 0:1024],
                                     HpreG[:, m, 0:1], None, ALU.mult)
                vector.drain()
                for n in range(1, 4):
                    vector.scalar_tensor_tensor(
                        s_xpm[:, :], s_xnm[:, n * 1024:(n + 1) * 1024],
                        HpreG[:, m, n:n + 1], s_xpm[:, :], ALU.mult, ALU.add)
                    vector.drain()
                vector.engine_nop().then_inc(xp_v, 1)

            def av_yev(m):
                vector.wait_ge(ymm, 2 * (m + 1))
                if m >= 2:
                    vector.wait_ge(y_st, 16 * (m - 1))
                vector.tensor_copy(s_ym[:, m % 2, 0:512], pb[0][:, :])
                vector.tensor_copy(s_ym[:, m % 2, 512:1024], pb[1][:, :])
                vector.drain()
                vector.engine_nop().then_inc(y_ev, 1)

            def av_d5(m):
                vector.wait_ge(xn_ld, 16 * (NT + m + 1))
                vector.wait_ge(ym_ld, 16 * (m + 1))
                for i2 in range(4):
                    e = 4 * m + i2
                    if e >= 2:
                        vector.wait_ge(g_st, 16 * (2 * (e - 2) + 2))
                    vector.tensor_scalar(s_fm[:, e % 2, :], s_xnm[:, 0:1024],
                                         HresQ[:, m, i2, 0:1], None, ALU.mult)
                    vector.drain()
                    for j2 in range(1, 4):
                        vector.scalar_tensor_tensor(
                            s_fm[:, e % 2, :],
                            s_xnm[:, j2 * 1024:(j2 + 1) * 1024],
                            HresQ[:, m, i2, j2:j2 + 1], s_fm[:, e % 2, :],
                            ALU.mult, ALU.add)
                        vector.drain()
                    vector.scalar_tensor_tensor(
                        s_fm[:, e % 2, :], s_ym[:, m % 2, :],
                        HpostG[:, m, i2:i2 + 1], s_fm[:, e % 2, :],
                        ALU.mult, ALU.add)
                    vector.drain()
                    vector.engine_nop().then_inc(d5_v, 1)

            def aux_pre(r):
                if r in Y_SLOTS:
                    m0, m1 = Y_SLOTS[r]
                    for m in range(m0, m1):
                        av_yev(m)

            def aux_vector(r):
                if r == R_D1 and not state["d1"]:
                    av_d1z(); av_gs(); av_gsev(); state["d1"] = True
                    return
                if r == R_D1 + 1 and not state["smax"]:
                    av_smax(); state["smax"] = True
                if r >= R_D1 + 2 and not state["sinit"]:
                    av_sinit(); state["sinit"] = True
                if r >= R_D1 + 2 and state["sk"] < SINK_IT:
                    n = min(2, SINK_IT - state["sk"])
                    for _ in range(n):
                        av_sk_iter()
                    state["sk"] += n
                if 10 <= r <= 26 and state["xpre"] < NT:
                    av_xpre(state["xpre"]); state["xpre"] += 1
                if r >= 53 and state["sk"] >= SINK_IT and not state["hfin"]:
                    av_hfin(); state["hfin"] = True
                if r >= 54 and state["hfin"] and state["d5"] < NT:
                    av_d5(state["d5"]); state["d5"] += 1

            ppc_i = 0
            for r, (c, mat, grp) in enumerate(ROUNDS):
                W = CHW[c]
                aux_pre(r)
                if mat == "u":
                    for ts_ in range(8):
                        bank = ts_
                        vector.wait_ge(a_mm, 8 * r + ts_ + 1)
                        vector.wait_ge(a_slr, 8 * G_IDX[r - 1] + ts_ + 1)
                        if ppc_i >= 2:
                            vector.wait_ge(a_pst, 16 * (ppc_i - 1))
                        vector.tensor_tensor(
                            s_ppc[:, ppc_i % 2, 0:W],
                            s_sl[:, ts_, 0:W], pb[bank][:, 0:W], ALU.mult)
                        vector.drain()
                        vector.engine_nop().then_inc(a_pr, 1)
                        ppc_i += 1
                        vector.scalar_tensor_tensor(
                            s_sl[:, ts_, 0:W], s_sl[:, ts_, 0:W], -1.0,
                            s_sl[:, ts_, 0:W], ALU.mult, ALU.max)
                        vector.drain()
                        vector.engine_nop().then_inc(a_ar, 1)
                aux_vector(r)
            while state["sk"] < SINK_IT:
                av_sk_iter(); state["sk"] += 1
            if not state["hfin"]:
                av_hfin(); state["hfin"] = True
            while state["d5"] < NT:
                av_d5(state["d5"]); state["d5"] += 1

            bstate = {"piece": 0, "nbcnt": 0}

            def emit_evac():
                p = bstate["piece"]
                heg = p // 4
                he = heg % NHE
                ts_ = p % 4
                bankbase = 0 if he % 2 == 0 else 4
                bank = bankbase + ts_
                vector.wait_ge(c_blk, NBLK_HE * (heg + 1))
                vector.wait_ge(c_xg0 if p % 2 == 0 else c_xg1,
                               32 * (p // 2 + 1))
                if p >= 2:
                    vector.wait_ge(c_os0 if p % 2 == 0 else c_os1,
                                   16 * (p // 2))
                vector.tensor_tensor(s_outc[:, p % 2, :], pb[bank][:, :],
                                     s_gpc[:, p % 2, :], ALU.mult).then_inc(c_ev, 1)
                vector.drain()
                vector.tensor_tensor(s_outc[:, p % 2, :], s_outc[:, p % 2, :],
                                     s_xpc[:, p % 2, :], ALU.add).then_inc(c_fin, 1)
                vector.drain()
                bstate["piece"] += 1

            def pieces_unlocked(m):
                return 32 * (m // 4)

            for m in range(NT):
                par = m % 2
                vector.wait_ge(b_lda, 16 * (m + 1))
                vector.memset(s_bmid[:, par:par + 1], HI0 / 2)
                vector.memset(s_bstp[:, par:par + 1], HI0 / 4)
                vector.drain()
                vector.engine_nop().then_inc(b_mid, 1)
                for it in range(N_IT):
                    if it % 2 == 0:
                        vector.tensor_scalar(s_btrv[:, :], s_ba[:, 0:IHB],
                                             s_bmid[:, par:par + 1], 0.0,
                                             ALU.is_le, ALU.add,
                                             accum_out=s_bcnt[:, par:par + 1])
                        vector.drain()
                        vector.tensor_scalar(s_btrv[:, :], s_ba[:, IHB:I],
                                             s_bmid[:, par:par + 1], 0.0,
                                             ALU.is_le, ALU.add,
                                             accum_out=s_st0[:, 12 + par:13 + par])
                        vector.drain()
                        vector.tensor_tensor(s_bcnt[:, par:par + 1],
                                             s_bcnt[:, par:par + 1],
                                             s_st0[:, 12 + par:13 + par],
                                             ALU.add)
                        vector.drain()
                    else:
                        bstate["nbcnt"] += 1
                        vector.wait_ge(b_cnt, bstate["nbcnt"])
                        vector.tensor_tensor(s_bcnt[:, par:par + 1],
                                             s_st0[:, 8 + 2 * par:9 + 2 * par],
                                             s_st0[:, 9 + 2 * par:10 + 2 * par],
                                             ALU.add)
                        vector.drain()
                        vector.tensor_scalar(s_bcnt[:, par:par + 1],
                                             s_bcnt[:, par:par + 1], 0.5,
                                             float(I) / 2, ALU.mult, ALU.add)
                        vector.drain()
                    vector.tensor_scalar(s_bsel[:, par:par + 1],
                                         s_bcnt[:, par:par + 1], float(K_TOP),
                                         None, ALU.is_ge)
                    vector.drain()
                    vector.tensor_scalar(s_bsel[:, par:par + 1],
                                         s_bsel[:, par:par + 1], -2.0, 1.0,
                                         ALU.mult, ALU.add)
                    vector.drain()
                    vector.scalar_tensor_tensor(s_bmid[:, par:par + 1],
                                                s_bsel[:, par:par + 1],
                                                s_bstp[:, par:par + 1],
                                                s_bmid[:, par:par + 1],
                                                ALU.mult, ALU.add)
                    vector.tensor_scalar(s_bstp[:, par:par + 1],
                                         s_bstp[:, par:par + 1], 0.5, None,
                                         ALU.mult)
                    vector.drain()
                    if it < N_IT - 1:
                        vector.engine_nop().then_inc(b_mid, 1)
                    if it % 2 == 1 and bstate["piece"] < pieces_unlocked(m):
                        emit_evac()
                for q in range(NPQ):
                    pidx = NPQ * m + q
                    vector.wait_ge(b_lp0 if pidx % 2 == 0 else b_lp1,
                                   16 * (pidx // 2 + 1))
                    if pidx >= 2:
                        sp = (pidx - 2) % 2
                        vector.wait_ge(b_st0 if sp == 0 else b_st1,
                                       16 * ((pidx - 2) // 2 + 1))
                    vector.scalar_tensor_tensor(
                        s_bsr[:, pidx % 2, :],
                        s_ba[:, q * IQ:(q + 1) * IQ],
                        s_bmid[:, par:par + 1],
                        s_bpr[:, pidx % 2, :],
                        ALU.is_gt, ALU.mult).then_inc(b_smr, 1)
                    vector.drain()
            while bstate["piece"] < NPIECE:
                emit_evac()

    return nc


def kernel(L0_attn_x, L0_attn_w_gate, L0_attn_w_up, L0_attn_w_down,
           L0_ffn_W_res, L0_ffn_W_pre, L0_ffn_W_post, L0_ffn_H_res_base,
           L0_ffn_H_pre_base, L0_ffn_H_post_base, L0_ffn_gate, L0_ffn_W_branch):
    x = np.ascontiguousarray(np.asarray(L0_attn_x, np.float32).reshape(NCORES * T, H))
    wgT = np.ascontiguousarray(np.asarray(L0_attn_w_gate, np.float32).T)
    wuT = np.ascontiguousarray(np.asarray(L0_attn_w_up, np.float32).T)
    wdT = np.ascontiguousarray(np.asarray(L0_attn_w_down, np.float32).T)
    wcT = np.ascontiguousarray(
        np.concatenate([np.asarray(L0_ffn_W_res, np.float32),
                        np.asarray(L0_ffn_W_pre, np.float32),
                        np.asarray(L0_ffn_W_post, np.float32)], 0).T)
    wbr = np.ascontiguousarray(np.asarray(L0_ffn_W_branch, np.float32))
    bases = np.ascontiguousarray(np.tile(
        np.concatenate([np.asarray(L0_ffn_H_res_base, np.float32).reshape(16),
                        np.asarray(L0_ffn_H_pre_base, np.float32).reshape(4),
                        np.asarray(L0_ffn_H_post_base, np.float32).reshape(4)]),
        (128, 1)))
    eyeb = np.ascontiguousarray(np.tile(np.eye(4, dtype=np.float32).reshape(16),
                                        (128, 1)))
    onesb = np.ones((1, 128), dtype=bf16)
    gate = np.ascontiguousarray(np.asarray(L0_ffn_gate, np.float32).reshape(1, 1))

    nc = build_program()
    in_maps = []
    for ci in range(NCORES):
        in_maps.append({
            "x": np.ascontiguousarray(x[ci * T:(ci + 1) * T]),
            "wgT": wgT, "wuT": wuT, "wdT": wdT, "wcT": wcT, "wbr": wbr,
            "bases": bases, "eyeb": eyeb, "onesb": onesb, "gatev": gate,
        })
    res = run_bass_kernel_spmd(nc, in_maps, core_ids=list(range(NCORES)))
    out = np.concatenate([np.asarray(r["out"], np.float32) for r in res.results], 0)
    return out.reshape(4, 4096, 4096)
